# version 07
# baseline (speedup 1.0000x reference)
import numpy as np
import concourse.bass as bass
import concourse.mybir as mybir
from concourse.bass_utils import run_bass_kernel_spmd

F32 = mybir.dt.float32
BF16 = mybir.dt.bfloat16
AX = mybir.AxisListType.X
ALU = mybir.AluOpType
ACT = mybir.ActivationFunctionType

NCORES = 8
D = 4096
B = 2
SEQ = 8192
CSL = SEQ // NCORES
T = B * CSL
NT = T // 128
NPAIR = 8
LC = D // 128
NOC = 8
NCH = 8
SCALE = float(SEQ) ** 0.5
INV_D = 1.0 / D


def _addr(nc, handle):
    for a in nc.m.functions[0].allocations:
        if isinstance(a, mybir.MemoryLocationSet) and a.memorylocations:
            m = a.memorylocations[0]
            if m.name == handle.name:
                return m.addr
    raise KeyError(handle.name)


def build_nc():
    nc = bass.Bass("TRN2", target_bir_lowering=False)

    x_d = nc.dram_tensor("x", [T, D], F32, kind="ExternalInput")
    wT_d = nc.dram_tensor("wT", [D, D], F32, kind="ExternalInput")
    gw_d = nc.dram_tensor("gw", [8, D], F32, kind="ExternalInput")
    bias_d = nc.dram_tensor("bias", [D], F32, kind="ExternalInput")
    a2t_d = nc.dram_tensor("a2t", [D, 128], F32, kind="ExternalInput")
    b2_d = nc.dram_tensor("b2", [128, D], F32, kind="ExternalInput")
    y_d = nc.dram_tensor("y", [T, D], F32, kind="ExternalOutput")
    ar_in_d = nc.dram_tensor("ar_in", [128, 96], F32)
    ar_out_d = nc.dram_tensor("ar_out", [128, 96], F32)
    wh_scr = nc.dram_tensor("wh_scr", [NOC, 4, 128, 4096], BF16)
    wl_scr = nc.dram_tensor("wl_scr", [NOC, 4, 128, 4096], BF16)

    A = nc.alloc_sbuf_tensor
    x1 = A("x1", [128, 2 * D], F32)
    fm = A("fm", [128, 2 * D], F32)
    res = A("res", [128, 2 * D], F32)
    wbuf = A("wbuf", [128, 2 * D], F32)
    lnt = A("lnt", [128, D], F32)
    a2th = A("a2th", [128, D], BF16)
    a2tl = A("a2tl", [128, D], BF16)
    b2h = A("b2h", [128, D], BF16)
    b2l = A("b2l", [128, D], BF16)
    bias_b = A("bias_b", [128, D], F32)
    gwh = A("gwh", [128, 256], BF16)
    gwl = A("gwl", [128, 256], BF16)
    ident = A("ident", [128, 128], F32)
    repmat = A("repmat", [8, 128], F32)
    coliota = A("coliota", [128, 128], F32)
    rowiota = A("rowiota", [128, 1], F32)
    colio8 = A("colio8", [8, 128], F32)
    eps_s = A("eps_s", [128, 1], F32)
    mu1 = A("mu1", [128, NT], F32)
    rs1 = A("rs1", [128, NT], F32)
    partials = A("partials", [128, 96 * 8], F32)
    scores_g = A("scores_g", [128, 96], F32)
    p01 = A("p01", [128, LC], F32)
    p10 = A("p10", [128, LC], F32)
    p00p1 = A("p00p1", [128, LC], F32)
    p11p1 = A("p11p1", [128, LC], F32)
    dtmp = A("dtmp", [128, 128], F32)
    stat = A("stat", [128, 24], F32)
    rsum_s = A("rsum_s", [128, 16], F32)
    rssq_s = A("rssq_s", [128, 16], F32)
    ltm_s = A("ltm_s", [128, 16], F32)
    l8tmp = A("l8tmp", [128, 8], F32)
    wftm = A("wftm", [128, 16], F32)
    sch = A("sch", [128, 256], BF16)
    scl = A("scl", [128, 256], BF16)

    res_a = _addr(nc, res)
    fm_a = _addr(nc, fm)
    lnt_a = _addr(nc, lnt)
    wbuf_a = _addr(nc, wbuf)
    AT = nc.alloc_sbuf_tensor_at
    aot = AT("aot", [128, D], F32, offset=res_a)
    aof = AT("aof", [128, 512], F32, offset=res_a + 4 * D)
    aog = AT("aog", [128, 1024], F32, offset=res_a + 4 * D + 2048)
    st_h = AT("st_h", [128, 4096], BF16, offset=res_a)
    st_l = AT("st_l", [128, 4096], BF16, offset=res_a + 8192)
    wfp = AT("wfp", [128, 4096], F32, offset=res_a + 16384)
    b2f = AT("b2f", [128, D], F32, offset=fm_a)
    a2f = AT("a2f", [128, D], F32, offset=fm_a + 4 * D)
    gwnat = AT("gwnat", [8, D], F32, offset=lnt_a)
    ln2h = AT("ln2h", [128, 2 * D], BF16, offset=fm_a)
    ln2l = AT("ln2l", [128, 2 * D], BF16, offset=fm_a + 4 * D)
    le_s = AT("le_s", [8, 256], F32, offset=lnt_a + 2048)
    wfT = AT("wfT", [8, 256], F32, offset=lnt_a + 3072)
    wsel_s = AT("wsel_s", [128, 256], F32, offset=lnt_a + 4096)
    scf = AT("scf", [128, 256], F32, offset=lnt_a + 5120)
    wbh = AT("wbh", [128, 2 * D], BF16, offset=wbuf_a)
    wbl = AT("wbl", [128, 2 * D], BF16, offset=wbuf_a + 4 * D)
    junk = lnt

    P = nc.alloc_psum_tensor
    bank = [P(f"bk{i}", [128, 512], F32) for i in range(4)]
    b4 = P("b4", [128, 512], F32)
    b5 = P("b5", [128, 512], F32)
    b67 = [P("b6", [128, 512], F32), P("b7", [128, 512], F32)]

    sem = {k: nc.alloc_semaphore(name="sm_" + k)
           for k in ["qx", "qw0", "qw1", "qp0", "qp1", "qpw", "qg", "qo",
                     "v", "s", "t", "g", "cc", "kb"]}
    C = {k: 0 for k in sem}

    def inc(inst, k, n=1):
        inst.then_inc(sem[k], n)
        C[k] += n
        return C[k]

    def w(eng, k, val=None):
        eng.wait_ge(sem[k], C[k] if val is None else val)

    marks = {}

    def mark(name, k):
        marks[name] = C[k]

    ts = bass.AP

    with nc.Block() as block:
        def BV(f):
            block.vector(lambda v: (w(v, "v"), f(v)) and None)

        def BS(f):
            block.scalar(lambda s: (w(s, "s"), f(s)) and None)

        def BG(f):
            block.gpsimd(lambda g: (w(g, "g"), f(g)) and None)

        def _setup_sync(sync):
            inc(sync.dma_start(out=gwnat[:, :], in_=gw_d[:, :]), "qx", 16)
            inc(sync.dma_start(out=bias_b[:, :],
                               in_=bias_d.ap().partition_broadcast(128)), "qx", 16)
            inc(sync.dma_start(out=a2f[:, :],
                               in_=ts(a2t_d, 0, [[128, 128], [16384, 32], [1, 128]])), "qx", 16)
            inc(sync.dma_start(out=b2f[:, :], in_=b2_d[:, :]), "qx", 16)
        block.sync(_setup_sync)

        def _setup_g(g):
            g.iota(coliota[:, :], [[1, 128]], channel_multiplier=0,
                   allow_small_or_imprecise_dtypes=True)
            g.iota(rowiota[:, :], [[0, 1]], channel_multiplier=1,
                   allow_small_or_imprecise_dtypes=True)
            inc(g.iota(colio8[:, :], [[1, 128]], channel_multiplier=-16,
                       allow_small_or_imprecise_dtypes=True), "g")
        BG(_setup_g)

        def _setup_v(v):
            w(v, "qx")
            w(v, "g")
            v.memset(eps_s[:, :], 1e-5)
            v.tensor_scalar(ident[:, :], coliota[:, :], rowiota[:, :], None, ALU.is_equal)
            _ld = v.tensor_scalar(repmat[:, :], colio8[:, :], -0.5, None, ALU.is_ge)
            inc(_ld, "v"); w(v, "v")
            v.scalar_tensor_tensor(repmat[:, :], colio8[:, :], 15.5, repmat[:, :],
                                   ALU.is_le, ALU.mult)
            _ld = v.tensor_scalar_mul(b2h[:, :], b2f[:, :], 2.0)
            inc(_ld, "v"); w(v, "v")
            v.scalar_tensor_tensor(b2l[:, :], b2f[:, :], 2.0, b2h[:, :],
                                   ALU.mult, ALU.subtract)
            _ld = v.tensor_copy(a2th[:, :], a2f[:, :])
            inc(_ld, "v"); w(v, "v")
            _li = v.tensor_tensor(a2tl[:, :], a2f[:, :], a2th[:, :], ALU.subtract)
            inc(_li, "v")
        BV(_setup_v)
        mark("setup_v", "v")

        def _setup_t(t):
            w(t, "v", marks["setup_v"])
            for k in range(LC):
                ti = t.transpose(b5[:, k * 8:(k + 1) * 8],
                                 gwnat[0:8, k * 128:(k + 1) * 128], ident[0:8, 0:8])
            inc(ti, "t")
        block.tensor(_setup_t)
        mark("setup_t", "t")

        def _setup_v2(v):
            w(v, "t", marks["setup_t"])
            _ld = v.tensor_copy(gwh[:, :], b5[:, 0:256])
            inc(_ld, "v"); w(v, "v")
            _li = v.tensor_tensor(gwl[:, :], b5[:, 0:256], gwh[:, :], ALU.subtract)
            inc(_li, "v")
        BV(_setup_v2)
        mark("gw_done", "v")

        def ln_stats(src_ap, mu_ap, rs_ap, c):
            twait = C["t"]

            def _v1(v):
                v.drain()
                _li = v.tensor_reduce(stat[:, c:c + 1], src_ap, AX, ALU.add)
                inc(_li, "v")
            BV(_v1)
            mv = C["v"]

            def _s1(s):
                w(s, "v", mv)
                w(s, "t", twait)
                _li = s.activation(junk[:, 0:D], src_ap, ACT.Square,
                             accum_out=stat[:, c + 1:c + 2])
                inc(_li, "s")
            BS(_s1)
            msq = C["s"]

            def _v2(v):
                w(v, "s", msq)
                v.tensor_scalar_mul(mu_ap, stat[:, c:c + 1], INV_D)
                _ld = v.tensor_scalar_mul(stat[:, c + 2:c + 3], stat[:, c + 1:c + 2], INV_D)
                inc(_ld, "v"); w(v, "v")
                _ld = v.tensor_tensor(stat[:, c + 3:c + 4], mu_ap, mu_ap, ALU.mult)
                inc(_ld, "v"); w(v, "v")
                _li = v.tensor_tensor(stat[:, c + 2:c + 3], stat[:, c + 2:c + 3],
                                stat[:, c + 3:c + 4], ALU.subtract)
                inc(_li, "v")
            BV(_v2)
            mvar = C["v"]

            def _s2(s):
                w(s, "v", mvar)
                _li = s.activation(stat[:, c + 3:c + 4], stat[:, c + 2:c + 3], ACT.Sqrt,
                             bias=eps_s[:, :])
                inc(_li, "s")
            BS(_s2)
            msd = C["s"]

            def _v3(v):
                w(v, "s", msd)
                _li = v.reciprocal(rs_ap, stat[:, c + 3:c + 4])
                inc(_li, "v")
            BV(_v3)

        tr_state = {"last_evac": [marks["gw_done"], marks["gw_done"]],
                    "last_s": [0, 0]}

        def transpose_tile_to_fm(src_tensor, col_of_lc, apply_mark):
            for grp in range(8):
                par = grp % 2

                def _t(t, grp=grp, par=par):
                    w(t, "v", apply_mark)
                    w(t, "v", tr_state["last_evac"][par])
                    w(t, "s", tr_state["last_s"][par])
                    for j in range(4):
                        lc = grp * 4 + j
                        ti = t.transpose(b67[par][:, j * 128:(j + 1) * 128],
                                         src_tensor[:, lc * 128:(lc + 1) * 128],
                                         ident[:, :])
                    inc(ti, "t")
                block.tensor(_t)
                mt = C["t"]

                def _v(v, grp=grp, par=par, mt=mt):
                    w(v, "t", mt)
                    lc0 = grp * 4
                    dst = ts(fm, col_of_lc(lc0), [[2 * D, 128], [256, 4], [1, 128]])
                    inc(v.tensor_copy(dst, b67[par][:, :]), "v")
                BV(_v)
                tr_state["last_evac"][par] = C["v"]

        x_consumed_v = C["v"]
        x_consumed_s = C["s"]
        prep_rounds = [(oc, kb) for oc in range(NOC) for kb in range(4)]
        prep_i = 0
        marks["prep_wr"] = 0

        def emit_prep_round(oc, kb):
            def _ld(sync, oc=oc, kb=kb):
                sync.wait_ge(sem["g"], marks.get("prep_g", 0))
                src = ts(wT_d, (kb * 8 * 128) * D + oc * 512,
                         [[D, 128], [128 * D, 8], [1, 512]])
                inc(sync.dma_start(out=wfp[:, :], in_=src), "qp0", 16)
            block.sync(_ld)
            mq = C["qp0"]

            def _g(g, oc=oc, kb=kb, mq=mq):
                w(g, "qp0", mq)
                w(g, "qpw", marks.get("prep_wr", 0))
                _ld = g.tensor_copy(st_h[:, :], wfp[:, :])
                inc(_ld, "g"); w(g, "g")
                _li = g.tensor_tensor(st_l[:, :], wfp[:, :], st_h[:, :], ALU.subtract)
                inc(_li, "g")
                w(g, "g")
                g.dma_start(out=wh_scr[oc, kb], in_=st_h[:, :]).then_inc(sem["qpw"], 16)
                g.dma_start(out=wl_scr[oc, kb], in_=st_l[:, :]).then_inc(sem["qpw"], 16)
            BG(_g)
            C["qpw"] += 32
            marks["prep_g"] = C["g"]
            marks[f"scr_{oc}_{kb}"] = C["qpw"]
            marks["prep_wr"] = C["qpw"]

        def ln_stats_pair(src0, src1, twait=None):
            tw = C["t"] if twait is None else twait

            def _v1(v):
                v.drain()
                v.tensor_reduce(stat[:, 0:1], src0, AX, ALU.add)
                _li = v.tensor_reduce(stat[:, 1:2], src1, AX, ALU.add)
                inc(_li, "v")
            BV(_v1)
            mv = C["v"]

            def _s1(s, mv=mv, tw=tw):
                w(s, "v", mv)
                w(s, "t", tw)
                _ld = s.activation(junk[:, 0:D], src0, ACT.Square, accum_out=stat[:, 2:3])
                inc(_ld, "s"); w(s, "s")
                _li = s.activation(junk[:, 0:D], src1, ACT.Square, accum_out=stat[:, 3:4])
                inc(_li, "s")
            BS(_s1)
            msq = C["s"]

            def _v2(v, msq=msq):
                w(v, "s", msq)
                _ld = v.tensor_scalar_mul(stat[:, 4:6], stat[:, 0:2], INV_D)
                inc(_ld, "v"); w(v, "v")
                v.tensor_scalar_mul(stat[:, 8:10], stat[:, 2:4], INV_D)
                _ld = v.tensor_tensor(stat[:, 10:12], stat[:, 4:6], stat[:, 4:6], ALU.mult)
                inc(_ld, "v"); w(v, "v")
                _li = v.tensor_tensor(stat[:, 8:10], stat[:, 8:10], stat[:, 10:12],
                                ALU.subtract)
                inc(_li, "v")
            BV(_v2)
            mvar = C["v"]

            def _s2(s, mvar=mvar):
                w(s, "v", mvar)
                _li = s.activation(stat[:, 10:12], stat[:, 8:10], ACT.Sqrt, bias=eps_s[:, :])
                inc(_li, "s")
            BS(_s2)
            msd = C["s"]

            def _v3(v, msd=msd):
                w(v, "s", msd)
                _li = v.reciprocal(stat[:, 6:8], stat[:, 10:12])
                inc(_li, "v")
            BV(_v3)

        junk1 = wbuf

        mx_next = None
        for p in range(NPAIR):
            if p == 0:
                xcv, xcs = x_consumed_v, x_consumed_s

                def _ldx(sync, p=p, xcv=xcv, xcs=xcs):
                    w(sync, "v", xcv)
                    w(sync, "s", xcs)
                    inc(sync.dma_start(out=x1[:, 0:D],
                                       in_=x_d[p * 128:(p + 1) * 128, :]), "qx", 16)
                    inc(sync.dma_start(out=x1[:, D:2 * D],
                                       in_=x_d[(8 + p) * 128:(9 + p) * 128, :]), "qx", 16)
                block.sync(_ldx)
                mx_next = C["qx"]
            mx = mx_next

            def _wx(v, mx=mx):
                w(v, "qx", mx)
                inc(v.drain(), "v")
            BV(_wx)

            def _wxs(s, mx=mx):
                w(s, "qx", mx)
                inc(s.drain(), "s")
            BS(_wxs)

            tw = C["t"]
            src0 = x1[:, 0:D]
            src1 = x1[:, D:2 * D]
            def _v1(v, src0=src0, src1=src1):
                v.drain()
                v.tensor_reduce(stat[:, 0:1], src0, AX, ALU.add)
                _li = v.tensor_reduce(stat[:, 1:2], src1, AX, ALU.add)
                inc(_li, "v")
            BV(_v1)
            mv = C["v"]

            def _s1(s, mv=mv, src0=src0, src1=src1):
                w(s, "v", mv)
                _ld = s.activation(junk1[:, 0:D], src0, ACT.Square, accum_out=stat[:, 2:3])
                inc(_ld, "s"); w(s, "s")
                _li = s.activation(junk1[:, D:2 * D], src1, ACT.Square, accum_out=stat[:, 3:4])
                inc(_li, "s")
            BS(_s1)
            msq = C["s"]

            def _v2(v, p=p, msq=msq):
                w(v, "s", msq)
                _ld = v.tensor_scalar_mul(stat[:, 4:6], stat[:, 0:2], INV_D)
                inc(_ld, "v"); w(v, "v")
                v.tensor_scalar_mul(stat[:, 8:10], stat[:, 2:4], INV_D)
                _ld = v.tensor_tensor(stat[:, 10:12], stat[:, 4:6], stat[:, 4:6], ALU.mult)
                inc(_ld, "v"); w(v, "v")
                _li = v.tensor_tensor(stat[:, 8:10], stat[:, 8:10], stat[:, 10:12],
                                ALU.subtract)
                inc(_li, "v")
            BV(_v2)
            mvar = C["v"]

            def _s2(s, mvar=mvar):
                w(s, "v", mvar)
                _li = s.activation(stat[:, 10:12], stat[:, 8:10], ACT.Sqrt, bias=eps_s[:, :])
                inc(_li, "s")
            BS(_s2)
            msd = C["s"]

            def _v3(v, p=p, msd=msd):
                w(v, "s", msd)
                _ld = v.reciprocal(stat[:, 6:8], stat[:, 10:12])
                inc(_ld, "v"); w(v, "v")
                v.tensor_copy(mu1[:, p:p + 1], stat[:, 4:5])
                v.tensor_copy(mu1[:, 8 + p:9 + p], stat[:, 5:6])
                v.tensor_copy(rs1[:, p:p + 1], stat[:, 6:7])
                _li = v.tensor_copy(rs1[:, 8 + p:9 + p], stat[:, 7:8])
                inc(_li, "v")
            BV(_v3)

            for b in range(2):
                g = p + 8 * b
                src = x1[:, b * D:(b + 1) * D]

                def _apply(v, src=src, g=g):
                    _li = v.tensor_scalar(lnt[:, 0:D], src, mu1[:, g:g + 1], rs1[:, g:g + 1],
                                    ALU.subtract, ALU.mult)
                    inc(_li, "v")
                BV(_apply)
                transpose_tile_to_fm(lnt, lambda lc, b=b: lc * 256 + b * 128, C["v"])

            x_consumed_v = C["v"]
            x_consumed_s = C["s"]
            if p + 1 < NPAIR:
                xcv, xcs = x_consumed_v, x_consumed_s

                def _ldx2(sync, p=p, xcv=xcv, xcs=xcs):
                    w(sync, "v", xcv)
                    w(sync, "s", xcs)
                    inc(sync.dma_start(out=x1[:, 0:D],
                                       in_=x_d[(p + 1) * 128:(p + 2) * 128, :]), "qx", 16)
                    inc(sync.dma_start(out=x1[:, D:2 * D],
                                       in_=x_d[(9 + p) * 128:(10 + p) * 128, :]), "qx", 16)
                block.sync(_ldx2)
                mx_next = C["qx"]

            for _ in range(4):
                if prep_i < len(prep_rounds):
                    emit_prep_round(*prep_rounds[prep_i])
                    prep_i += 1

            mqfm = C["v"]

            def _sqp(s, p=p, mqfm=mqfm):
                w(s, "v", mqfm)
                q0 = ts(fm, 0, [[2 * D, 128], [256, 32], [1, 128]])
                q1 = ts(fm, 128, [[2 * D, 128], [256, 32], [1, 128]])
                s.drain()
                _ld = s.activation(junk1[:, 0:D], q0, ACT.Square)
                inc(_ld, "s"); w(s, "s")
                _li = s.activation(junk1[:, D:2 * D], q1, ACT.Square)
                inc(_li, "s")
            BS(_sqp)
            msq2 = C["s"]

            def _prod(v, p=p, msq2=msq2):
                q0 = ts(fm, 0, [[2 * D, 128], [256, 32], [1, 128]])
                q1 = ts(fm, 128, [[2 * D, 128], [256, 32], [1, 128]])
                v.drain()
                _ld = v.tensor_tensor(lnt[:, 0:D], q0, q1, ALU.mult)
                inc(_ld, "v"); w(v, "v")
                w(v, "s", msq2)
                for pair, (tsr, off) in ((0, (junk1, 0)), (1, (lnt, 0)),
                                         (2, (junk1, D))):
                    srcap = ts(tsr, off, [[tsr.shape[1], 128], [128, 32], [1, 128]])
                    dst = ts(partials, pair * 256 + p, [[96 * 8, 128], [8, 32]])
                    _li = v.tensor_reduce(dst, srcap, AX, ALU.add)
                inc(_li, "v")
            BV(_prod)
            marks["p1_prod_v"] = C["v"]
            marks["p1_sq_s"] = C["s"]

        while prep_i < len(prep_rounds):
            emit_prep_round(*prep_rounds[prep_i])
            prep_i += 1

        def _red(v):
            v.drain()
            _li = v.tensor_reduce(scores_g[:, :],
                            ts(partials, 0, [[96 * 8, 128], [8, 96], [1, 8]]),
                            AX, ALU.add)
            inc(_li, "v")
        BV(_red)
        mark("scores_local", "v")

        def _ar(g):
            w(g, "v", marks["scores_local"])
            inc(g.dma_start(out=ar_in_d[:, :], in_=scores_g[:, :]), "qg", 16)
            w(g, "qg")
            inc(g.collective_compute("AllReduce", ALU.add,
                                     replica_groups=[list(range(NCORES))],
                                     ins=[ar_in_d.ap().opt()],
                                     outs=[ar_out_d.ap().opt()]), "cc")
            w(g, "cc")
            inc(g.dma_start(out=scores_g[:, :], in_=ar_out_d[:, :]), "qg", 16)
        BG(_ar)
        mar = C["qg"]

        def _arows_v(v):
            w(v, "qg", mar)
            v.tensor_tensor(dtmp[:, 0:32], scores_g[:, 32:64], scores_g[:, 0:32],
                            ALU.subtract)
            _ld = v.tensor_tensor(dtmp[:, 32:64], scores_g[:, 32:64], scores_g[:, 64:96],
                            ALU.subtract)
            inc(_ld, "v"); w(v, "v")
            _ld = v.tensor_scalar_mul(dtmp[:, 0:64], dtmp[:, 0:64], 1.0 / SCALE)
            inc(_ld, "v"); w(v, "v")
            _li = v.tensor_scalar_mul(dtmp[:, 64:128], dtmp[:, 0:64], -1.0)
            inc(_li, "v")
        BV(_arows_v)
        mdv = C["v"]

        def _arows_s(s):
            w(s, "v", mdv)
            s.activation(p01[:, :], dtmp[:, 0:32], ACT.Sigmoid)
            s.activation(p10[:, :], dtmp[:, 32:64], ACT.Sigmoid)
            s.activation(p00p1[:, :], dtmp[:, 64:96], ACT.Sigmoid)
            _li = s.activation(p11p1[:, :], dtmp[:, 96:128], ACT.Sigmoid)
            inc(_li, "s")
        BS(_arows_s)
        mas = C["s"]

        def _arows_v2(v):
            w(v, "s", mas)
            v.tensor_scalar_add(p00p1[:, :], p00p1[:, :], 1.0)
            _li = v.tensor_scalar_add(p11p1[:, :], p11p1[:, :], 1.0)
            inc(_li, "v")
        BV(_arows_v2)

        mprep_wr = marks["prep_wr"]
        mprep_g = marks.get("prep_g", 0)

        def _prep_fence(v, mprep_wr=mprep_wr, mprep_g=mprep_g):
            w(v, "qpw", mprep_wr)
            w(v, "g", mprep_g)
            inc(v.drain(), "v")
        BV(_prep_fence)
        mark("arows", "v")

        bank_free = [marks["arows"]] * 4
        b4_free = marks["arows"]
        wseq = 0
        out_done = 0

        for n in range(NCH):
            g0, g1 = n, 8 + n
            xcv, xcs, od = x_consumed_v, x_consumed_s, out_done

            def _ldx(sync, n=n, xcv=xcv, xcs=xcs, od=od):
                w(sync, "v", xcv)
                w(sync, "s", xcs)
                w(sync, "qo", od)
                inc(sync.dma_start(out=x1[:, 0:D], in_=x_d[n * 128:(n + 1) * 128, :]),
                    "qx", 16)
                inc(sync.dma_start(out=x1[:, D:2 * D],
                                   in_=x_d[(8 + n) * 128:(9 + n) * 128, :]), "qx", 16)
            block.sync(_ldx)
            mx = C["qx"]

            def _wpre(sc, n=n, wbase=wseq):
                if n == 0:
                    sc.wait_ge(sem["s"], marks["p1_sq_s"])
                    sc.wait_ge(sem["v"], marks["p1_prod_v"])
                for wv in range(32):
                    oc, kb = divmod(wv, 4)
                    seqi = wbase + wv
                    par = seqi % 2
                    if seqi >= 2:
                        sc.wait_ge(sem["kb"], seqi - 1)
                    if n == 0:
                        sc.wait_ge(sem["qpw"], marks[f"scr_{oc}_{kb}"])
                    inc(sc.dma_start(out=wbh[:, par * D:(par + 1) * D],
                                     in_=wh_scr[oc, kb]), f"qw{par}", 16)
                    inc(sc.dma_start(out=wbl[:, par * D:(par + 1) * D],
                                     in_=wl_scr[oc, kb]), f"qw{par}", 16)
            block.sync(_wpre)
            wseq += 32

            def _wx(v, mx=mx):
                w(v, "qx", mx)
                inc(v.drain(), "v")
            BV(_wx)

            def _wxs(s, mx=mx):
                w(s, "qx", mx)
                inc(s.drain(), "s")
            BS(_wxs)

            for b in range(2):
                g = (g0, g1)[b]

                def _apply1(v, b=b, g=g):
                    _li = v.tensor_scalar(lnt[:, 0:D], x1[:, b * D:(b + 1) * D],
                                    mu1[:, g:g + 1], rs1[:, g:g + 1],
                                    ALU.subtract, ALU.mult)
                    inc(_li, "v")
                BV(_apply1)
                transpose_tile_to_fm(lnt, lambda lc, b=b: lc * 256 + b * 128, C["v"])

            for b in range(2):
                pmul = p01 if b == 0 else p10
                pself = p00p1 if b == 0 else p11p1
                for grp in range(8):
                    par = grp % 2

                    def _comb(v, b=b, grp=grp, par=par, pmul=pmul, pself=pself):
                        w(v, "t", marks.get(f"aog{par}", 0))
                        for j in range(4):
                            lc = grp * 4 + j
                            qo = fm[:, lc * 256 + (1 - b) * 128: lc * 256 + (1 - b) * 128 + 128]
                            _ld = v.tensor_scalar_mul(aof[:, j * 128:(j + 1) * 128], qo,
                                                pmul[:, lc:lc + 1])
                        inc(_ld, "v"); w(v, "v")
                        for j in range(4):
                            lc = grp * 4 + j
                            qs = fm[:, lc * 256 + b * 128: lc * 256 + b * 128 + 128]
                            _li = v.scalar_tensor_tensor(
                                aog[:, par * 512 + j * 128: par * 512 + (j + 1) * 128],
                                qs, pself[:, lc:lc + 1], aof[:, j * 128:(j + 1) * 128],
                                ALU.mult, ALU.add)
                        inc(_li, "v")
                    BV(_comb)
                    mc = C["v"]

                    def _trb(t, grp=grp, par=par, mc=mc):
                        w(t, "v", mc)
                        w(t, "v", tr_state["last_evac"][par])
                        for j in range(4):
                            ti = t.transpose(
                                b67[par][:, j * 128:(j + 1) * 128],
                                aog[:, par * 512 + j * 128: par * 512 + (j + 1) * 128],
                                ident[:, :])
                        inc(ti, "t")
                    block.tensor(_trb)
                    marks[f"aog{par}"] = C["t"]
                    mt = C["t"]

                    def _evb(v, grp=grp, par=par, mt=mt):
                        w(v, "t", mt)
                        inc(v.tensor_copy(aot[:, grp * 512:(grp + 1) * 512],
                                          b67[par][:, :]), "v")
                    BV(_evb)
                    tr_state["last_evac"][par] = C["v"]

                ln_stats(aot[:, 0:D], stat[:, 4:5], stat[:, 5:6], 0)

                def _addres(v, b=b):
                    _ld = v.tensor_scalar(aot[:, 0:D], aot[:, 0:D], stat[:, 4:5], stat[:, 5:6],
                                    ALU.subtract, ALU.mult)
                    inc(_ld, "v"); w(v, "v")
                    _li = v.tensor_tensor(x1[:, b * D:(b + 1) * D], x1[:, b * D:(b + 1) * D],
                                    aot[:, 0:D], ALU.add)
                    inc(_li, "v")
                BV(_addres)

            ln_stats_pair(x1[:, 0:D], x1[:, D:2 * D])
            for b in range(2):
                def _apply2(v, b=b):
                    _li = v.tensor_scalar(lnt[:, 0:D], x1[:, b * D:(b + 1) * D],
                                    stat[:, 4 + b:5 + b], stat[:, 6 + b:7 + b],
                                    ALU.subtract, ALU.mult)
                    inc(_li, "v")
                BV(_apply2)
                apply_mark = C["v"]
                for grp in range(8):
                    par = grp % 2

                    def _t(t, grp=grp, par=par, apply_mark=apply_mark):
                        w(t, "v", apply_mark)
                        w(t, "v", tr_state["last_evac"][par])
                        w(t, "s", tr_state["last_s"][par])
                        for j in range(4):
                            lc = grp * 4 + j
                            ti = t.transpose(b67[par][:, j * 128:(j + 1) * 128],
                                             lnt[:, lc * 128:(lc + 1) * 128],
                                             ident[:, :])
                        inc(ti, "t")
                    block.tensor(_t)
                    mt = C["t"]

                    def _sh(s, grp=grp, par=par, b=b, mt=mt):
                        w(s, "t", mt)
                        lc0 = grp * 4
                        dst = ts(ln2h, lc0 * 256 + b * 128,
                                 [[2 * D, 128], [256, 4], [1, 128]])
                        s.drain()
                        _li = s.activation(dst, b67[par][:, :], ACT.Copy)
                        inc(_li, "s")
                    BS(_sh)
                    tr_state["last_s"][par] = C["s"]
                    msh = C["s"]

                    def _vl(v, grp=grp, par=par, b=b, mt=mt, msh=msh):
                        w(v, "t", mt)
                        w(v, "s", msh)
                        lc0 = grp * 4
                        dsth = ts(ln2h, lc0 * 256 + b * 128,
                                  [[2 * D, 128], [256, 4], [1, 128]])
                        dstl = ts(ln2l, lc0 * 256 + b * 128,
                                  [[2 * D, 128], [256, 4], [1, 128]])
                        inc(v.tensor_tensor(dstl, b67[par][:, :], dsth, ALU.subtract),
                            "v")
                    BV(_vl)
                    tr_state["last_evac"][par] = C["v"]

            mfm = C["v"]
            bf = b4_free

            def emit_mains(oc):
                bks = [2 * (oc % 2), 2 * (oc % 2) + 1]
                bfr = [bank_free[bks[0]], bank_free[bks[1]]]

                def _main(t, n=n, oc=oc, bks=bks, bfr=bfr, mfm=mfm):
                    w(t, "v", mfm)
                    w(t, "v", max(bfr))
                    for kb in range(4):
                        seqi = n * 32 + oc * 4 + kb
                        par = seqi % 2
                        t.wait_ge(sem[f"qw{par}"], 32 * (seqi // 2 + 1))
                        for k8 in range(8):
                            k = kb * 8 + k8
                            wh_sl = wbh[:, par * D + k8 * 512: par * D + (k8 + 1) * 512]
                            wl_sl = wbl[:, par * D + k8 * 512: par * D + (k8 + 1) * 512]
                            for tt in range(2):
                                h_sl = ln2h[:, k * 256 + tt * 128: k * 256 + tt * 128 + 128]
                                l_sl = ln2l[:, k * 256 + tt * 128: k * 256 + tt * 128 + 128]
                                t.matmul(bank[bks[tt]][:, :], h_sl, wh_sl,
                                         start=(kb == 0 and k8 == 0), stop=False)
                                t.matmul(bank[bks[tt]][:, :], h_sl, wl_sl,
                                         start=False, stop=False)
                                ti = t.matmul(bank[bks[tt]][:, :], l_sl, wh_sl,
                                              start=False, stop=False)
                        inc(ti, "kb")
                block.tensor(_main)

            def _gate(t, mfm=mfm, bf=bf):
                w(t, "v", mfm)
                w(t, "v", bf)
                first = True
                for pi, (sta, mov) in enumerate(((gwh, ln2h), (gwh, ln2l), (gwl, ln2h))):
                    for k in range(LC):
                        ti = t.matmul(b4[0:8, 0:256], sta[:, k * 8:(k + 1) * 8],
                                      mov[:, k * 256:(k + 1) * 256],
                                      start=first, stop=(pi == 2 and k == LC - 1))
                        first = False
                inc(ti, "t")
            block.tensor(_gate)
            mg = C["t"]

            emit_mains(0)

            def _ev_le(v, mg=mg):
                w(v, "t", mg)
                inc(v.tensor_copy(le_s[:, :], b4[0:8, 0:256]), "v")
            BV(_ev_le)
            mle = C["v"]

            def _tr_l(t, mle=mle):
                w(t, "v", mle)
                for tt in range(2):
                    ti = t.transpose(b5[:, tt * 8:(tt + 1) * 8],
                                     le_s[0:8, tt * 128:(tt + 1) * 128], ident[0:8, 0:8])
                inc(ti, "t")
            block.tensor(_tr_l)
            mtl = C["t"]

            emit_mains(1)

            def _top2(v, mtl=mtl):
                w(v, "t", mtl)
                _ld = v.tensor_copy(ltm_s[:, :], b5[:, 0:16])
                inc(_ld, "v"); w(v, "v")
                for tt in range(2):
                    sl = ltm_s[:, tt * 8:(tt + 1) * 8]
                    _ld = v.max(out=l8tmp[:, :], in_=sl)
                    inc(_ld, "v"); w(v, "v")
                    v.tensor_copy(stat[:, 12 + tt:13 + tt], l8tmp[:, 0:1])
                    _ld = v.tensor_copy(stat[:, 14 + tt:15 + tt], l8tmp[:, 1:2])
                    inc(_ld, "v"); w(v, "v")
                _ld = v.tensor_tensor(stat[:, 16:18], stat[:, 12:14], stat[:, 14:16],
                                ALU.subtract)
                inc(_ld, "v"); w(v, "v")
                _li = v.tensor_scalar_mul(stat[:, 18:20], stat[:, 16:18], -1.0)
                inc(_li, "v")
            BV(_top2)
            mt2 = C["v"]

            def _sig(s, mt2=mt2):
                w(s, "v", mt2)
                s.activation(stat[:, 20:22], stat[:, 16:18], ACT.Sigmoid)
                _li = s.activation(stat[:, 22:24], stat[:, 18:20], ACT.Sigmoid)
                inc(_li, "s")
            BS(_sig)
            msg_ = C["s"]

            def _wf(v, msg_=msg_):
                w(v, "s", msg_)
                for tt in range(2):
                    sl = ltm_s[:, tt * 8:(tt + 1) * 8]
                    _ld = v.tensor_scalar(l8tmp[:, :], sl, stat[:, 12 + tt:13 + tt], None,
                                    ALU.is_equal)
                    inc(_ld, "v"); w(v, "v")
                    _ld = v.tensor_scalar_mul(wftm[:, tt * 8:(tt + 1) * 8], l8tmp[:, :],
                                        stat[:, 20 + tt:21 + tt])
                    inc(_ld, "v"); w(v, "v")
                    _ld = v.tensor_scalar(l8tmp[:, :], sl, stat[:, 14 + tt:15 + tt], None,
                                    ALU.is_equal)
                    inc(_ld, "v"); w(v, "v")
                    _ld = v.scalar_tensor_tensor(wftm[:, tt * 8:(tt + 1) * 8], l8tmp[:, :],
                                           stat[:, 22 + tt:23 + tt],
                                           wftm[:, tt * 8:(tt + 1) * 8],
                                           ALU.mult, ALU.add)
                    inc(_ld, "v"); w(v, "v")
                inc(v.drain(), "v")
            BV(_wf)
            mwf = C["v"]

            def _tr_wf(t, mwf=mwf):
                w(t, "v", mwf)
                for tt in range(2):
                    ti = t.transpose(b5[0:8, 256 + tt * 128:256 + (tt + 1) * 128],
                                     wftm[:, tt * 8:(tt + 1) * 8], ident[:, :])
                inc(ti, "t")
            block.tensor(_tr_wf)
            mtw = C["t"]

            def _ev_wf(v, mtw=mtw):
                w(v, "t", mtw)
                inc(v.tensor_copy(wfT[:, :], b5[0:8, 256:512]), "v")
            BV(_ev_wf)
            mewf = C["v"]

            def _wselmm(t, mewf=mewf):
                w(t, "v", mewf)
                ti = t.matmul(b4[:, 256:512], repmat[:, :], wfT[:, :],
                              start=True, stop=True)
                inc(ti, "t")
            block.tensor(_wselmm)
            mws = C["t"]

            def _ev_ws(v, mws=mws):
                w(v, "t", mws)
                inc(v.tensor_copy(wsel_s[:, :], b4[:, 256:512]), "v")
            BV(_ev_ws)
            mews = C["v"]

            def _lowmm(t, mews=mews):
                w(t, "v", mews)
                first = True
                for pi, (sta, mov) in enumerate(((a2th, ln2h), (a2th, ln2l), (a2tl, ln2h))):
                    for k in range(LC):
                        ti = t.matmul(b4[:, 0:256], sta[:, k * 128:(k + 1) * 128],
                                      mov[:, k * 256:(k + 1) * 256],
                                      start=first, stop=(pi == 2 and k == LC - 1))
                        first = False
                inc(ti, "t")
            block.tensor(_lowmm)
            mlow = C["t"]

            def _scaled(v, mlow=mlow):
                w(v, "t", mlow)
                _ld = v.tensor_tensor(scf[:, :], b4[:, 0:256], wsel_s[:, :], ALU.mult)
                inc(_ld, "v"); w(v, "v")
                _ld = v.tensor_copy(sch[:, :], scf[:, :])
                inc(_ld, "v"); w(v, "v")
                _li = v.tensor_tensor(scl[:, :], scf[:, :], sch[:, :], ALU.subtract)
                inc(_li, "v")
            BV(_scaled)
            b4_free = C["v"]
            msc = C["v"]

            for oc in range(NOC):
                bks = [2 * (oc % 2), 2 * (oc % 2) + 1]

                def _close(t, oc=oc, bks=bks, msc=msc):
                    w(t, "v", msc)
                    for tt in range(2):
                        t.matmul(bank[bks[tt]][:, :], sch[:, tt * 128:(tt + 1) * 128],
                                 b2h[:, oc * 512:(oc + 1) * 512], start=False, stop=False)
                        t.matmul(bank[bks[tt]][:, :], sch[:, tt * 128:(tt + 1) * 128],
                                 b2l[:, oc * 512:(oc + 1) * 512], start=False, stop=False)
                        ti = t.matmul(bank[bks[tt]][:, :], scl[:, tt * 128:(tt + 1) * 128],
                                      b2h[:, oc * 512:(oc + 1) * 512], start=False,
                                      stop=True)
                    inc(ti, "t")
                block.tensor(_close)
                mcl = C["t"]

                def _evac(v, oc=oc, bks=bks, mcl=mcl):
                    w(v, "t", mcl)
                    for tt in range(2):
                        _li = v.scalar_tensor_tensor(
                            res[:, tt * D + oc * 512: tt * D + (oc + 1) * 512],
                            bank[bks[tt]][:, :], 1.0, bias_b[:, oc * 512:(oc + 1) * 512],
                            ALU.mult, ALU.add,
                            accum_out=rsum_s[:, tt * 8 + oc: tt * 8 + oc + 1])
                    inc(_li, "v")
                BV(_evac)
                bank_free[bks[0]] = bank_free[bks[1]] = C["v"]
                mev = C["v"]

                def _sq(s, oc=oc, mev=mev):
                    w(s, "v", mev)
                    for tt in range(2):
                        _ld = s.activation(junk[:, 0:512],
                                     res[:, tt * D + oc * 512: tt * D + (oc + 1) * 512],
                                     ACT.Square,
                                     accum_out=rssq_s[:, tt * 8 + oc: tt * 8 + oc + 1])
                        inc(_ld, "s"); w(s, "s")
                    inc(s.drain(), "s")
                BS(_sq)

                if oc + 2 < NOC:
                    emit_mains(oc + 2)

            msqs = C["s"]

            def _ep_v(v, msqs=msqs):
                w(v, "s", msqs)
                v.drain()
                v.tensor_reduce(stat[:, 0:1], rsum_s[:, 0:8], AX, ALU.add)
                v.tensor_reduce(stat[:, 1:2], rsum_s[:, 8:16], AX, ALU.add)
                v.tensor_reduce(stat[:, 2:3], rssq_s[:, 0:8], AX, ALU.add)
                _ld = v.tensor_reduce(stat[:, 3:4], rssq_s[:, 8:16], AX, ALU.add)
                inc(_ld, "v"); w(v, "v")
                _ld = v.tensor_scalar_mul(stat[:, 4:6], stat[:, 0:2], INV_D)
                inc(_ld, "v"); w(v, "v")
                v.tensor_scalar_mul(stat[:, 8:10], stat[:, 2:4], INV_D)
                _ld = v.tensor_tensor(stat[:, 10:12], stat[:, 4:6], stat[:, 4:6], ALU.mult)
                inc(_ld, "v"); w(v, "v")
                _li = v.tensor_tensor(stat[:, 8:10], stat[:, 8:10], stat[:, 10:12],
                                ALU.subtract)
                inc(_li, "v")
            BV(_ep_v)
            mev2 = C["v"]

            def _ep_s(s, mev2=mev2):
                w(s, "v", mev2)
                _li = s.activation(stat[:, 10:12], stat[:, 8:10], ACT.Sqrt, bias=eps_s[:, :])
                inc(_li, "s")
            BS(_ep_s)
            mes = C["s"]

            def _ep_v2(v, mes=mes):
                w(v, "s", mes)
                _ld = v.reciprocal(stat[:, 6:8], stat[:, 10:12])
                inc(_ld, "v"); w(v, "v")
                for tt in range(2):
                    _ld = v.tensor_scalar(res[:, tt * D:(tt + 1) * D],
                                    res[:, tt * D:(tt + 1) * D],
                                    stat[:, 4 + tt:5 + tt], stat[:, 6 + tt:7 + tt],
                                    ALU.subtract, ALU.mult)
                inc(_ld, "v"); w(v, "v")
                for tt in range(2):
                    _li = v.tensor_tensor(x1[:, tt * D:(tt + 1) * D],
                                    x1[:, tt * D:(tt + 1) * D],
                                    res[:, tt * D:(tt + 1) * D], ALU.add)
                inc(_li, "v")
            BV(_ep_v2)
            mout = C["v"]

            def _out(g, n=n, mout=mout):
                w(g, "v", mout)
                inc(g.dma_start(out=y_d[n * 128:(n + 1) * 128, :],
                                in_=x1[:, 0:D]), "qo", 16)
                inc(g.dma_start(out=y_d[(8 + n) * 128:(9 + n) * 128, :],
                                in_=x1[:, D:2 * D]), "qo", 16)
            BG(_out)
            x_consumed_v = C["v"]
            x_consumed_s = C["s"]
            out_done = C["qo"]

        def _fin(sync):
            w(sync, "qo")
        block.sync(_fin)

    return nc


def kernel(L0_attn_x, L0_ffn_gate_w, L0_ffn_base_w, L0_ffn_base_b,
           L0_ffn_lora_A, L0_ffn_lora_B):
    x = np.ascontiguousarray(np.asarray(L0_attn_x, dtype=np.float32))
    gw = np.ascontiguousarray(np.asarray(L0_ffn_gate_w, dtype=np.float32))
    bw = np.asarray(L0_ffn_base_w, dtype=np.float32)
    bb = np.ascontiguousarray(np.asarray(L0_ffn_base_b, dtype=np.float32))
    lA = np.asarray(L0_ffn_lora_A, dtype=np.float32)
    lB = np.asarray(L0_ffn_lora_B, dtype=np.float32)

    wT = np.ascontiguousarray(bw.T)
    a2t = np.ascontiguousarray(lA.transpose(2, 0, 1).reshape(D, 128))
    b2 = np.ascontiguousarray(lB.transpose(0, 2, 1).reshape(128, D))

    nc = build_nc()
    in_maps = []
    for i in range(NCORES):
        xc = np.ascontiguousarray(x[:, i * CSL:(i + 1) * CSL, :].reshape(T, D))
        in_maps.append({"x": xc, "wT": wT, "gw": gw, "bias": bb,
                        "a2t": a2t, "b2": b2})
    resu = run_bass_kernel_spmd(nc, in_maps, core_ids=list(range(NCORES)))
    out = np.empty((B, SEQ, D), dtype=np.float32)
    for i in range(NCORES):
        out[:, i * CSL:(i + 1) * CSL, :] = np.asarray(
            resu.results[i]["y"]).reshape(B, CSL, D)
    return out


# version 08
# speedup vs baseline: 1.0015x; 1.0015x over previous
import numpy as np
import concourse.bass as bass
import concourse.mybir as mybir
from concourse.bass_utils import run_bass_kernel_spmd

F32 = mybir.dt.float32
BF16 = mybir.dt.bfloat16
AX = mybir.AxisListType.X
ALU = mybir.AluOpType
ACT = mybir.ActivationFunctionType

NCORES = 8
D = 4096
B = 2
SEQ = 8192
CSL = SEQ // NCORES
T = B * CSL
NT = T // 128
NPAIR = 8
LC = D // 128
NOC = 8
NCH = 8
SCALE = float(SEQ) ** 0.5
INV_D = 1.0 / D


def _addr(nc, handle):
    for a in nc.m.functions[0].allocations:
        if isinstance(a, mybir.MemoryLocationSet) and a.memorylocations:
            m = a.memorylocations[0]
            if m.name == handle.name:
                return m.addr
    raise KeyError(handle.name)


def build_nc():
    nc = bass.Bass("TRN2", target_bir_lowering=False)

    x_d = nc.dram_tensor("x", [T, D], F32, kind="ExternalInput")
    wT_d = nc.dram_tensor("wT", [D, D], F32, kind="ExternalInput")
    gw_d = nc.dram_tensor("gw", [8, D], F32, kind="ExternalInput")
    bias_d = nc.dram_tensor("bias", [D], F32, kind="ExternalInput")
    a2t_d = nc.dram_tensor("a2t", [D, 128], F32, kind="ExternalInput")
    b2_d = nc.dram_tensor("b2", [128, D], F32, kind="ExternalInput")
    y_d = nc.dram_tensor("y", [T, D], F32, kind="ExternalOutput")
    ar_in_d = nc.dram_tensor("ar_in", [128, 96], F32)
    ar_out_d = nc.dram_tensor("ar_out", [128, 96], F32)
    wh_scr = nc.dram_tensor("wh_scr", [NOC, 4, 128, 4096], BF16)
    wl_scr = nc.dram_tensor("wl_scr", [NOC, 4, 128, 4096], BF16)

    A = nc.alloc_sbuf_tensor
    x1 = A("x1", [128, 2 * D], F32)
    fm = A("fm", [128, 2 * D], F32)
    res = A("res", [128, 2 * D], F32)
    wbuf = A("wbuf", [128, 2 * D], F32)
    lnt = A("lnt", [128, D], F32)
    a2th = A("a2th", [128, D], BF16)
    a2tl = A("a2tl", [128, D], BF16)
    b2h = A("b2h", [128, D], BF16)
    b2l = A("b2l", [128, D], BF16)
    bias_b = A("bias_b", [128, D], F32)
    gwh = A("gwh", [128, 256], BF16)
    gwl = A("gwl", [128, 256], BF16)
    ident = A("ident", [128, 128], F32)
    repmat = A("repmat", [8, 128], F32)
    coliota = A("coliota", [128, 128], F32)
    rowiota = A("rowiota", [128, 1], F32)
    colio8 = A("colio8", [8, 128], F32)
    eps_s = A("eps_s", [128, 1], F32)
    mu1 = A("mu1", [128, NT], F32)
    rs1 = A("rs1", [128, NT], F32)
    partials = A("partials", [128, 96 * 8], F32)
    scores_g = A("scores_g", [128, 96], F32)
    p01 = A("p01", [128, LC], F32)
    p10 = A("p10", [128, LC], F32)
    p00p1 = A("p00p1", [128, LC], F32)
    p11p1 = A("p11p1", [128, LC], F32)
    dtmp = A("dtmp", [128, 128], F32)
    stat = A("stat", [128, 24], F32)
    rsum_s = A("rsum_s", [128, 16], F32)
    rssq_s = A("rssq_s", [128, 16], F32)
    ltm_s = A("ltm_s", [128, 16], F32)
    l8tmp = A("l8tmp", [128, 8], F32)
    wftm = A("wftm", [128, 16], F32)
    sch = A("sch", [128, 256], BF16)
    scl = A("scl", [128, 256], BF16)

    res_a = _addr(nc, res)
    fm_a = _addr(nc, fm)
    lnt_a = _addr(nc, lnt)
    wbuf_a = _addr(nc, wbuf)
    AT = nc.alloc_sbuf_tensor_at
    aot = AT("aot", [128, D], F32, offset=res_a)
    aof = AT("aof", [128, 512], F32, offset=res_a + 4 * D)
    aog = AT("aog", [128, 1024], F32, offset=res_a + 4 * D + 2048)
    st_h = AT("st_h", [128, 4096], BF16, offset=res_a)
    st_l = AT("st_l", [128, 4096], BF16, offset=res_a + 8192)
    wfp = AT("wfp", [128, 4096], F32, offset=res_a + 16384)
    b2f = AT("b2f", [128, D], F32, offset=fm_a)
    a2f = AT("a2f", [128, D], F32, offset=fm_a + 4 * D)
    gwnat = AT("gwnat", [8, D], F32, offset=lnt_a)
    ln2h = AT("ln2h", [128, 2 * D], BF16, offset=fm_a)
    ln2l = AT("ln2l", [128, 2 * D], BF16, offset=fm_a + 4 * D)
    le_s = AT("le_s", [8, 256], F32, offset=lnt_a + 2048)
    wfT = AT("wfT", [8, 256], F32, offset=lnt_a + 3072)
    wsel_s = AT("wsel_s", [128, 256], F32, offset=lnt_a + 4096)
    scf = AT("scf", [128, 256], F32, offset=lnt_a + 5120)
    wbh = AT("wbh", [128, 2 * D], BF16, offset=wbuf_a)
    wbl = AT("wbl", [128, 2 * D], BF16, offset=wbuf_a + 4 * D)
    junk = lnt

    P = nc.alloc_psum_tensor
    bank = [P(f"bk{i}", [128, 512], F32) for i in range(4)]
    b4 = P("b4", [128, 512], F32)
    b5 = P("b5", [128, 512], F32)
    b67 = [P("b6", [128, 512], F32), P("b7", [128, 512], F32)]

    sem = {k: nc.alloc_semaphore(name="sm_" + k)
           for k in ["qx", "qw0", "qw1", "qp0", "qp1", "qpw", "qg", "qo",
                     "v", "s", "t", "g", "cc", "kb"]}
    C = {k: 0 for k in sem}

    def inc(inst, k, n=1):
        inst.then_inc(sem[k], n)
        C[k] += n
        return C[k]

    def w(eng, k, val=None):
        eng.wait_ge(sem[k], C[k] if val is None else val)

    marks = {}

    def mark(name, k):
        marks[name] = C[k]

    ts = bass.AP

    with nc.Block() as block:
        def BV(f):
            block.vector(lambda v: (w(v, "v"), f(v)) and None)

        def BS(f):
            block.scalar(lambda s: (w(s, "s"), f(s)) and None)

        def BG(f):
            block.gpsimd(lambda g: (w(g, "g"), f(g)) and None)

        def _setup_sync(sync):
            inc(sync.dma_start(out=gwnat[:, :], in_=gw_d[:, :]), "qx", 16)
            inc(sync.dma_start(out=bias_b[:, :],
                               in_=bias_d.ap().partition_broadcast(128)), "qx", 16)
            inc(sync.dma_start(out=a2f[:, :],
                               in_=ts(a2t_d, 0, [[128, 128], [16384, 32], [1, 128]])), "qx", 16)
            inc(sync.dma_start(out=b2f[:, :], in_=b2_d[:, :]), "qx", 16)
        block.sync(_setup_sync)

        def _setup_g(g):
            g.iota(coliota[:, :], [[1, 128]], channel_multiplier=0,
                   allow_small_or_imprecise_dtypes=True)
            g.iota(rowiota[:, :], [[0, 1]], channel_multiplier=1,
                   allow_small_or_imprecise_dtypes=True)
            inc(g.iota(colio8[:, :], [[1, 128]], channel_multiplier=-16,
                       allow_small_or_imprecise_dtypes=True), "g")
        BG(_setup_g)

        def _setup_v(v):
            w(v, "qx")
            w(v, "g")
            v.memset(eps_s[:, :], 1e-5)
            v.tensor_scalar(ident[:, :], coliota[:, :], rowiota[:, :], None, ALU.is_equal)
            _ld = v.tensor_scalar(repmat[:, :], colio8[:, :], -0.5, None, ALU.is_ge)
            inc(_ld, "v"); w(v, "v")
            v.scalar_tensor_tensor(repmat[:, :], colio8[:, :], 15.5, repmat[:, :],
                                   ALU.is_le, ALU.mult)
            _ld = v.tensor_scalar_mul(b2h[:, :], b2f[:, :], 2.0)
            inc(_ld, "v"); w(v, "v")
            v.scalar_tensor_tensor(b2l[:, :], b2f[:, :], 2.0, b2h[:, :],
                                   ALU.mult, ALU.subtract)
            _ld = v.tensor_copy(a2th[:, :], a2f[:, :])
            inc(_ld, "v"); w(v, "v")
            _li = v.tensor_tensor(a2tl[:, :], a2f[:, :], a2th[:, :], ALU.subtract)
            inc(_li, "v")
        BV(_setup_v)
        mark("setup_v", "v")

        def _setup_t(t):
            w(t, "v", marks["setup_v"])
            for k in range(LC):
                ti = t.transpose(b5[:, k * 8:(k + 1) * 8],
                                 gwnat[0:8, k * 128:(k + 1) * 128], ident[0:8, 0:8])
            inc(ti, "t")
        block.tensor(_setup_t)
        mark("setup_t", "t")

        def _setup_v2(v):
            w(v, "t", marks["setup_t"])
            _ld = v.tensor_copy(gwh[:, :], b5[:, 0:256])
            inc(_ld, "v"); w(v, "v")
            _li = v.tensor_tensor(gwl[:, :], b5[:, 0:256], gwh[:, :], ALU.subtract)
            inc(_li, "v")
        BV(_setup_v2)
        mark("gw_done", "v")

        def ln_stats(src_ap, mu_ap, rs_ap, c):
            twait = C["t"]

            def _v1(v):
                v.drain()
                _li = v.tensor_reduce(stat[:, c:c + 1], src_ap, AX, ALU.add)
                inc(_li, "v")
            BV(_v1)
            mv = C["v"]

            def _s1(s):
                w(s, "v", mv)
                w(s, "t", twait)
                _li = s.activation(junk[:, 0:D], src_ap, ACT.Square,
                             accum_out=stat[:, c + 1:c + 2])
                inc(_li, "s")
            BS(_s1)
            msq = C["s"]

            def _v2(v):
                w(v, "s", msq)
                v.tensor_scalar_mul(mu_ap, stat[:, c:c + 1], INV_D)
                _ld = v.tensor_scalar_mul(stat[:, c + 2:c + 3], stat[:, c + 1:c + 2], INV_D)
                inc(_ld, "v"); w(v, "v")
                _ld = v.tensor_tensor(stat[:, c + 3:c + 4], mu_ap, mu_ap, ALU.mult)
                inc(_ld, "v"); w(v, "v")
                _li = v.tensor_tensor(stat[:, c + 2:c + 3], stat[:, c + 2:c + 3],
                                stat[:, c + 3:c + 4], ALU.subtract)
                inc(_li, "v")
            BV(_v2)
            mvar = C["v"]

            def _s2(s):
                w(s, "v", mvar)
                _li = s.activation(stat[:, c + 3:c + 4], stat[:, c + 2:c + 3], ACT.Sqrt,
                             bias=eps_s[:, :])
                inc(_li, "s")
            BS(_s2)
            msd = C["s"]

            def _v3(v):
                w(v, "s", msd)
                _li = v.reciprocal(rs_ap, stat[:, c + 3:c + 4])
                inc(_li, "v")
            BV(_v3)

        tr_state = {"last_evac": [marks["gw_done"], marks["gw_done"]],
                    "last_s": [0, 0]}

        def transpose_tile_to_fm(src_tensor, col_of_lc, apply_mark):
            for grp in range(8):
                par = grp % 2

                def _t(t, grp=grp, par=par):
                    w(t, "v", apply_mark)
                    w(t, "v", tr_state["last_evac"][par])
                    w(t, "s", tr_state["last_s"][par])
                    for j in range(4):
                        lc = grp * 4 + j
                        ti = t.transpose(b67[par][:, j * 128:(j + 1) * 128],
                                         src_tensor[:, lc * 128:(lc + 1) * 128],
                                         ident[:, :])
                    inc(ti, "t")
                block.tensor(_t)
                mt = C["t"]

                def _v(v, grp=grp, par=par, mt=mt):
                    w(v, "t", mt)
                    lc0 = grp * 4
                    dst = ts(fm, col_of_lc(lc0), [[2 * D, 128], [256, 4], [1, 128]])
                    inc(v.tensor_copy(dst, b67[par][:, :]), "v")
                BV(_v)
                tr_state["last_evac"][par] = C["v"]

        x_consumed_v = C["v"]
        x_consumed_s = C["s"]
        prep_rounds = [(oc, kb) for oc in range(NOC) for kb in range(4)]
        prep_i = 0
        marks["prep_wr"] = 0

        def emit_prep_round(oc, kb):
            def _ld(sync, oc=oc, kb=kb):
                sync.wait_ge(sem["g"], marks.get("prep_g", 0))
                src = ts(wT_d, (kb * 8 * 128) * D + oc * 512,
                         [[D, 128], [128 * D, 8], [1, 512]])
                inc(sync.dma_start(out=wfp[:, :], in_=src), "qp0", 16)
            block.sync(_ld)
            mq = C["qp0"]

            def _g(g, oc=oc, kb=kb, mq=mq):
                w(g, "qp0", mq)
                w(g, "qpw", marks.get("prep_wr", 0))
                _ld = g.tensor_copy(st_h[:, :], wfp[:, :])
                inc(_ld, "g"); w(g, "g")
                _li = g.tensor_tensor(st_l[:, :], wfp[:, :], st_h[:, :], ALU.subtract)
                inc(_li, "g")
                w(g, "g")
                g.dma_start(out=wh_scr[oc, kb], in_=st_h[:, :]).then_inc(sem["qpw"], 16)
                g.dma_start(out=wl_scr[oc, kb], in_=st_l[:, :]).then_inc(sem["qpw"], 16)
            BG(_g)
            C["qpw"] += 32
            marks["prep_g"] = C["g"]
            marks[f"scr_{oc}_{kb}"] = C["qpw"]
            marks["prep_wr"] = C["qpw"]

        def ln_stats_pair(src0, src1, twait=None):
            tw = C["t"] if twait is None else twait

            def _v1(v):
                v.drain()
                v.tensor_reduce(stat[:, 0:1], src0, AX, ALU.add)
                _li = v.tensor_reduce(stat[:, 1:2], src1, AX, ALU.add)
                inc(_li, "v")
            BV(_v1)
            mv = C["v"]

            def _s1(s, mv=mv, tw=tw):
                w(s, "v", mv)
                w(s, "t", tw)
                _ld = s.activation(junk[:, 0:D], src0, ACT.Square, accum_out=stat[:, 2:3])
                inc(_ld, "s"); w(s, "s")
                _li = s.activation(junk[:, 0:D], src1, ACT.Square, accum_out=stat[:, 3:4])
                inc(_li, "s")
            BS(_s1)
            msq = C["s"]

            def _v2(v, msq=msq):
                w(v, "s", msq)
                _ld = v.tensor_scalar_mul(stat[:, 4:6], stat[:, 0:2], INV_D)
                inc(_ld, "v"); w(v, "v")
                v.tensor_scalar_mul(stat[:, 8:10], stat[:, 2:4], INV_D)
                _ld = v.tensor_tensor(stat[:, 10:12], stat[:, 4:6], stat[:, 4:6], ALU.mult)
                inc(_ld, "v"); w(v, "v")
                _li = v.tensor_tensor(stat[:, 8:10], stat[:, 8:10], stat[:, 10:12],
                                ALU.subtract)
                inc(_li, "v")
            BV(_v2)
            mvar = C["v"]

            def _s2(s, mvar=mvar):
                w(s, "v", mvar)
                _li = s.activation(stat[:, 10:12], stat[:, 8:10], ACT.Sqrt, bias=eps_s[:, :])
                inc(_li, "s")
            BS(_s2)
            msd = C["s"]

            def _v3(v, msd=msd):
                w(v, "s", msd)
                _li = v.reciprocal(stat[:, 6:8], stat[:, 10:12])
                inc(_li, "v")
            BV(_v3)

        junk1 = wbuf

        mx_next = None
        for p in range(NPAIR):
            if p == 0:
                xcv, xcs = x_consumed_v, x_consumed_s

                def _ldx(sync, p=p, xcv=xcv, xcs=xcs):
                    w(sync, "v", xcv)
                    w(sync, "s", xcs)
                    inc(sync.dma_start(out=x1[:, 0:D],
                                       in_=x_d[p * 128:(p + 1) * 128, :]), "qx", 16)
                    inc(sync.dma_start(out=x1[:, D:2 * D],
                                       in_=x_d[(8 + p) * 128:(9 + p) * 128, :]), "qx", 16)
                block.sync(_ldx)
                mx_next = C["qx"]
            mx = mx_next

            def _wx(v, mx=mx):
                w(v, "qx", mx)
                inc(v.drain(), "v")
            BV(_wx)

            def _wxs(s, mx=mx):
                w(s, "qx", mx)
                inc(s.drain(), "s")
            BS(_wxs)

            tw = C["t"]
            src0 = x1[:, 0:D]
            src1 = x1[:, D:2 * D]
            def _v1(v, src0=src0, src1=src1):
                v.drain()
                v.tensor_reduce(stat[:, 0:1], src0, AX, ALU.add)
                _li = v.tensor_reduce(stat[:, 1:2], src1, AX, ALU.add)
                inc(_li, "v")
            BV(_v1)
            mv = C["v"]

            def _s1(s, mv=mv, src0=src0, src1=src1):
                w(s, "v", mv)
                _ld = s.activation(junk1[:, 0:D], src0, ACT.Square, accum_out=stat[:, 2:3])
                inc(_ld, "s"); w(s, "s")
                _li = s.activation(junk1[:, D:2 * D], src1, ACT.Square, accum_out=stat[:, 3:4])
                inc(_li, "s")
            BS(_s1)
            msq = C["s"]

            def _v2(v, p=p, msq=msq):
                w(v, "s", msq)
                _ld = v.tensor_scalar_mul(stat[:, 4:6], stat[:, 0:2], INV_D)
                inc(_ld, "v"); w(v, "v")
                v.tensor_scalar_mul(stat[:, 8:10], stat[:, 2:4], INV_D)
                _ld = v.tensor_tensor(stat[:, 10:12], stat[:, 4:6], stat[:, 4:6], ALU.mult)
                inc(_ld, "v"); w(v, "v")
                _li = v.tensor_tensor(stat[:, 8:10], stat[:, 8:10], stat[:, 10:12],
                                ALU.subtract)
                inc(_li, "v")
            BV(_v2)
            mvar = C["v"]

            def _s2(s, mvar=mvar):
                w(s, "v", mvar)
                _li = s.activation(stat[:, 10:12], stat[:, 8:10], ACT.Sqrt, bias=eps_s[:, :])
                inc(_li, "s")
            BS(_s2)
            msd = C["s"]

            def _v3(v, p=p, msd=msd):
                w(v, "s", msd)
                _ld = v.reciprocal(stat[:, 6:8], stat[:, 10:12])
                inc(_ld, "v"); w(v, "v")
                v.tensor_copy(mu1[:, p:p + 1], stat[:, 4:5])
                v.tensor_copy(mu1[:, 8 + p:9 + p], stat[:, 5:6])
                v.tensor_copy(rs1[:, p:p + 1], stat[:, 6:7])
                _li = v.tensor_copy(rs1[:, 8 + p:9 + p], stat[:, 7:8])
                inc(_li, "v")
            BV(_v3)

            for b in range(2):
                g = p + 8 * b
                src = x1[:, b * D:(b + 1) * D]

                def _apply(v, src=src, g=g):
                    _li = v.tensor_scalar(lnt[:, 0:D], src, mu1[:, g:g + 1], rs1[:, g:g + 1],
                                    ALU.subtract, ALU.mult)
                    inc(_li, "v")
                BV(_apply)
                transpose_tile_to_fm(lnt, lambda lc, b=b: lc * 256 + b * 128, C["v"])

            x_consumed_v = C["v"]
            x_consumed_s = C["s"]
            if p + 1 < NPAIR:
                xcv, xcs = x_consumed_v, x_consumed_s

                def _ldx2(sync, p=p, xcv=xcv, xcs=xcs):
                    w(sync, "v", xcv)
                    w(sync, "s", xcs)
                    inc(sync.dma_start(out=x1[:, 0:D],
                                       in_=x_d[(p + 1) * 128:(p + 2) * 128, :]), "qx", 16)
                    inc(sync.dma_start(out=x1[:, D:2 * D],
                                       in_=x_d[(9 + p) * 128:(10 + p) * 128, :]), "qx", 16)
                block.sync(_ldx2)
                mx_next = C["qx"]

            for _ in range(4):
                if prep_i < len(prep_rounds):
                    emit_prep_round(*prep_rounds[prep_i])
                    prep_i += 1

            mqfm = C["v"]

            def _sqp(s, p=p, mqfm=mqfm):
                w(s, "v", mqfm)
                q0 = ts(fm, 0, [[2 * D, 128], [256, 32], [1, 128]])
                q1 = ts(fm, 128, [[2 * D, 128], [256, 32], [1, 128]])
                s.drain()
                _ld = s.activation(junk1[:, 0:D], q0, ACT.Square)
                inc(_ld, "s"); w(s, "s")
                _li = s.activation(junk1[:, D:2 * D], q1, ACT.Square)
                inc(_li, "s")
            BS(_sqp)
            msq2 = C["s"]

            def _prod(v, p=p, msq2=msq2):
                q0 = ts(fm, 0, [[2 * D, 128], [256, 32], [1, 128]])
                q1 = ts(fm, 128, [[2 * D, 128], [256, 32], [1, 128]])
                v.drain()
                _ld = v.tensor_tensor(lnt[:, 0:D], q0, q1, ALU.mult)
                inc(_ld, "v"); w(v, "v")
                w(v, "s", msq2)
                for pair, (tsr, off) in ((0, (junk1, 0)), (1, (lnt, 0)),
                                         (2, (junk1, D))):
                    srcap = ts(tsr, off, [[tsr.shape[1], 128], [128, 32], [1, 128]])
                    dst = ts(partials, pair * 256 + p, [[96 * 8, 128], [8, 32]])
                    _li = v.tensor_reduce(dst, srcap, AX, ALU.add)
                inc(_li, "v")
            BV(_prod)
            marks["p1_prod_v"] = C["v"]
            marks["p1_sq_s"] = C["s"]

        while prep_i < len(prep_rounds):
            emit_prep_round(*prep_rounds[prep_i])
            prep_i += 1

        def _red(v):
            v.drain()
            _li = v.tensor_reduce(scores_g[:, :],
                            ts(partials, 0, [[96 * 8, 128], [8, 96], [1, 8]]),
                            AX, ALU.add)
            inc(_li, "v")
        BV(_red)
        mark("scores_local", "v")

        def _ar(g):
            w(g, "v", marks["scores_local"])
            inc(g.dma_start(out=ar_in_d[:, :], in_=scores_g[:, :]), "qg", 16)
            w(g, "qg")
            inc(g.collective_compute("AllReduce", ALU.add,
                                     replica_groups=[list(range(NCORES))],
                                     ins=[ar_in_d.ap().opt()],
                                     outs=[ar_out_d.ap().opt()]), "cc")
            w(g, "cc")
            inc(g.dma_start(out=scores_g[:, :], in_=ar_out_d[:, :]), "qg", 16)
        BG(_ar)
        mar = C["qg"]

        def _arows_v(v):
            w(v, "qg", mar)
            v.tensor_tensor(dtmp[:, 0:32], scores_g[:, 32:64], scores_g[:, 0:32],
                            ALU.subtract)
            _ld = v.tensor_tensor(dtmp[:, 32:64], scores_g[:, 32:64], scores_g[:, 64:96],
                            ALU.subtract)
            inc(_ld, "v"); w(v, "v")
            _ld = v.tensor_scalar_mul(dtmp[:, 0:64], dtmp[:, 0:64], 1.0 / SCALE)
            inc(_ld, "v"); w(v, "v")
            _li = v.tensor_scalar_mul(dtmp[:, 64:128], dtmp[:, 0:64], -1.0)
            inc(_li, "v")
        BV(_arows_v)
        mdv = C["v"]

        def _arows_s(s):
            w(s, "v", mdv)
            s.activation(p01[:, :], dtmp[:, 0:32], ACT.Sigmoid)
            s.activation(p10[:, :], dtmp[:, 32:64], ACT.Sigmoid)
            s.activation(p00p1[:, :], dtmp[:, 64:96], ACT.Sigmoid)
            _li = s.activation(p11p1[:, :], dtmp[:, 96:128], ACT.Sigmoid)
            inc(_li, "s")
        BS(_arows_s)
        mas = C["s"]

        def _arows_v2(v):
            w(v, "s", mas)
            v.tensor_scalar_add(p00p1[:, :], p00p1[:, :], 1.0)
            _li = v.tensor_scalar_add(p11p1[:, :], p11p1[:, :], 1.0)
            inc(_li, "v")
        BV(_arows_v2)

        mprep_wr = marks["prep_wr"]
        mprep_g = marks.get("prep_g", 0)

        def _prep_fence(v, mprep_wr=mprep_wr, mprep_g=mprep_g):
            w(v, "qpw", mprep_wr)
            w(v, "g", mprep_g)
            inc(v.drain(), "v")
        BV(_prep_fence)
        mark("arows", "v")

        bank_free = [marks["arows"]] * 4
        b4_free = marks["arows"]
        wseq = 0
        out_done = 0

        for n in range(NCH):
            g0, g1 = n, 8 + n
            xcv, xcs, od = x_consumed_v, x_consumed_s, out_done

            def _ldx(sync, n=n, xcv=xcv, xcs=xcs, od=od):
                w(sync, "v", xcv)
                w(sync, "s", xcs)
                w(sync, "qo", od)
                inc(sync.dma_start(out=x1[:, 0:D], in_=x_d[n * 128:(n + 1) * 128, :]),
                    "qx", 16)
                inc(sync.dma_start(out=x1[:, D:2 * D],
                                   in_=x_d[(8 + n) * 128:(9 + n) * 128, :]), "qx", 16)
            block.sync(_ldx)
            mx = C["qx"]

            def _wpre(sc, n=n, wbase=wseq):
                if n == 0:
                    sc.wait_ge(sem["s"], marks["p1_sq_s"])
                    sc.wait_ge(sem["v"], marks["p1_prod_v"])
                for wv in range(32):
                    oc, kb = divmod(wv, 4)
                    seqi = wbase + wv
                    par = seqi % 2
                    if seqi >= 2:
                        sc.wait_ge(sem["kb"], seqi - 1)
                    if n == 0:
                        sc.wait_ge(sem["qpw"], marks[f"scr_{oc}_{kb}"])
                    inc(sc.dma_start(out=wbh[:, par * D:(par + 1) * D],
                                     in_=wh_scr[oc, kb]), f"qw{par}", 16)
                    inc(sc.dma_start(out=wbl[:, par * D:(par + 1) * D],
                                     in_=wl_scr[oc, kb]), f"qw{par}", 16)
            block.sync(_wpre)
            wseq += 32

            def _wx(v, mx=mx):
                w(v, "qx", mx)
                inc(v.drain(), "v")
            BV(_wx)

            def _wxs(s, mx=mx):
                w(s, "qx", mx)
                inc(s.drain(), "s")
            BS(_wxs)

            for b in range(2):
                g = (g0, g1)[b]

                def _apply1(v, b=b, g=g):
                    _li = v.tensor_scalar(lnt[:, 0:D], x1[:, b * D:(b + 1) * D],
                                    mu1[:, g:g + 1], rs1[:, g:g + 1],
                                    ALU.subtract, ALU.mult)
                    inc(_li, "v")
                BV(_apply1)
                transpose_tile_to_fm(lnt, lambda lc, b=b: lc * 256 + b * 128, C["v"])

            for b in range(2):
                pmul = p01 if b == 0 else p10
                pself = p00p1 if b == 0 else p11p1
                for grp in range(8):
                    par = grp % 2

                    def _comb(v, b=b, grp=grp, par=par, pmul=pmul, pself=pself):
                        w(v, "t", marks.get(f"aog{par}", 0))
                        for j in range(4):
                            lc = grp * 4 + j
                            qo = fm[:, lc * 256 + (1 - b) * 128: lc * 256 + (1 - b) * 128 + 128]
                            _ld = v.tensor_scalar_mul(aof[:, j * 128:(j + 1) * 128], qo,
                                                pmul[:, lc:lc + 1])
                        inc(_ld, "v"); w(v, "v")
                        for j in range(4):
                            lc = grp * 4 + j
                            qs = fm[:, lc * 256 + b * 128: lc * 256 + b * 128 + 128]
                            _li = v.scalar_tensor_tensor(
                                aog[:, par * 512 + j * 128: par * 512 + (j + 1) * 128],
                                qs, pself[:, lc:lc + 1], aof[:, j * 128:(j + 1) * 128],
                                ALU.mult, ALU.add)
                        inc(_li, "v")
                    BV(_comb)
                    mc = C["v"]

                    def _trb(t, grp=grp, par=par, mc=mc):
                        w(t, "v", mc)
                        w(t, "v", tr_state["last_evac"][par])
                        for j in range(4):
                            ti = t.transpose(
                                b67[par][:, j * 128:(j + 1) * 128],
                                aog[:, par * 512 + j * 128: par * 512 + (j + 1) * 128],
                                ident[:, :])
                        inc(ti, "t")
                    block.tensor(_trb)
                    marks[f"aog{par}"] = C["t"]
                    mt = C["t"]

                    def _evb(v, grp=grp, par=par, mt=mt):
                        w(v, "t", mt)
                        inc(v.tensor_copy(aot[:, grp * 512:(grp + 1) * 512],
                                          b67[par][:, :]), "v")
                    BV(_evb)
                    tr_state["last_evac"][par] = C["v"]

                ln_stats(aot[:, 0:D], stat[:, 4:5], stat[:, 5:6], 0)

                def _addres(v, b=b):
                    _ld = v.tensor_scalar(aot[:, 0:D], aot[:, 0:D], stat[:, 4:5], stat[:, 5:6],
                                    ALU.subtract, ALU.mult)
                    inc(_ld, "v"); w(v, "v")
                    _li = v.tensor_tensor(x1[:, b * D:(b + 1) * D], x1[:, b * D:(b + 1) * D],
                                    aot[:, 0:D], ALU.add)
                    inc(_li, "v")
                BV(_addres)

            ln_stats_pair(x1[:, 0:D], x1[:, D:2 * D])
            for b in range(2):
                def _apply2(v, b=b):
                    _li = v.tensor_scalar(lnt[:, 0:D], x1[:, b * D:(b + 1) * D],
                                    stat[:, 4 + b:5 + b], stat[:, 6 + b:7 + b],
                                    ALU.subtract, ALU.mult)
                    inc(_li, "v")
                BV(_apply2)
                apply_mark = C["v"]
                for grp in range(8):
                    par = grp % 2

                    def _t(t, grp=grp, par=par, apply_mark=apply_mark):
                        w(t, "v", apply_mark)
                        w(t, "v", tr_state["last_evac"][par])
                        w(t, "s", tr_state["last_s"][par])
                        for j in range(4):
                            lc = grp * 4 + j
                            ti = t.transpose(b67[par][:, j * 128:(j + 1) * 128],
                                             lnt[:, lc * 128:(lc + 1) * 128],
                                             ident[:, :])
                        inc(ti, "t")
                    block.tensor(_t)
                    mt = C["t"]

                    def _sh(s, grp=grp, par=par, b=b, mt=mt):
                        w(s, "t", mt)
                        lc0 = grp * 4
                        dst = ts(ln2h, lc0 * 256 + b * 128,
                                 [[2 * D, 128], [256, 4], [1, 128]])
                        s.drain()
                        _li = s.activation(dst, b67[par][:, :], ACT.Copy)
                        inc(_li, "s")
                    BS(_sh)
                    tr_state["last_s"][par] = C["s"]
                    msh = C["s"]

                    def _vl(v, grp=grp, par=par, b=b, mt=mt, msh=msh):
                        w(v, "t", mt)
                        w(v, "s", msh)
                        lc0 = grp * 4
                        dsth = ts(ln2h, lc0 * 256 + b * 128,
                                  [[2 * D, 128], [256, 4], [1, 128]])
                        dstl = ts(ln2l, lc0 * 256 + b * 128,
                                  [[2 * D, 128], [256, 4], [1, 128]])
                        inc(v.tensor_tensor(dstl, b67[par][:, :], dsth, ALU.subtract),
                            "v")
                    BV(_vl)
                    tr_state["last_evac"][par] = C["v"]

            mfm = C["v"]
            bf = b4_free

            def emit_mains(oc):
                bks = [2 * (oc % 2), 2 * (oc % 2) + 1]
                bfr = [bank_free[bks[0]], bank_free[bks[1]]]

                def _main(t, n=n, oc=oc, bks=bks, bfr=bfr, mfm=mfm):
                    w(t, "v", mfm)
                    w(t, "v", max(bfr))
                    for kb in range(4):
                        seqi = n * 32 + oc * 4 + kb
                        par = seqi % 2
                        t.wait_ge(sem[f"qw{par}"], 32 * (seqi // 2 + 1))
                        for k8 in range(8):
                            k = kb * 8 + k8
                            wh_sl = wbh[:, par * D + k8 * 512: par * D + (k8 + 1) * 512]
                            wl_sl = wbl[:, par * D + k8 * 512: par * D + (k8 + 1) * 512]
                            for tt in range(2):
                                h_sl = ln2h[:, k * 256 + tt * 128: k * 256 + tt * 128 + 128]
                                l_sl = ln2l[:, k * 256 + tt * 128: k * 256 + tt * 128 + 128]
                                t.matmul(bank[bks[tt]][:, :], h_sl, wh_sl,
                                         start=(kb == 0 and k8 == 0), stop=False)
                                t.matmul(bank[bks[tt]][:, :], h_sl, wl_sl,
                                         start=False, stop=False)
                                ti = t.matmul(bank[bks[tt]][:, :], l_sl, wh_sl,
                                              start=False, stop=False)
                        inc(ti, "kb")
                block.tensor(_main)

            def _gate(t, mfm=mfm, bf=bf):
                w(t, "v", mfm)
                w(t, "v", bf)
                first = True
                for pi, (sta, mov) in enumerate(((gwh, ln2h), (gwh, ln2l), (gwl, ln2h))):
                    for k in range(LC):
                        ti = t.matmul(b4[0:8, 0:256], sta[:, k * 8:(k + 1) * 8],
                                      mov[:, k * 256:(k + 1) * 256],
                                      start=first, stop=(pi == 2 and k == LC - 1))
                        first = False
                inc(ti, "t")
            block.tensor(_gate)
            mg = C["t"]

            emit_mains(0)

            def _ev_le(v, mg=mg):
                w(v, "t", mg)
                inc(v.tensor_copy(le_s[:, :], b4[0:8, 0:256]), "v")
            BV(_ev_le)
            mle = C["v"]

            def _tr_l(t, mle=mle):
                w(t, "v", mle)
                for tt in range(2):
                    ti = t.transpose(b5[:, tt * 8:(tt + 1) * 8],
                                     le_s[0:8, tt * 128:(tt + 1) * 128], ident[0:8, 0:8])
                inc(ti, "t")
            block.tensor(_tr_l)
            mtl = C["t"]

            emit_mains(1)

            def _top2(v, mtl=mtl):
                w(v, "t", mtl)
                _ld = v.tensor_copy(ltm_s[:, :], b5[:, 0:16])
                inc(_ld, "v"); w(v, "v")
                for tt in range(2):
                    sl = ltm_s[:, tt * 8:(tt + 1) * 8]
                    _ld = v.max(out=l8tmp[:, :], in_=sl)
                    inc(_ld, "v"); w(v, "v")
                    v.tensor_copy(stat[:, 12 + tt:13 + tt], l8tmp[:, 0:1])
                    _ld = v.tensor_copy(stat[:, 14 + tt:15 + tt], l8tmp[:, 1:2])
                    inc(_ld, "v"); w(v, "v")
                _ld = v.tensor_tensor(stat[:, 16:18], stat[:, 12:14], stat[:, 14:16],
                                ALU.subtract)
                inc(_ld, "v"); w(v, "v")
                _ld = v.tensor_scalar_mul(stat[:, 18:20], stat[:, 16:18], -1.0)
                inc(_ld, "v"); w(v, "v")
                _li = v.tensor_scalar_min(stat[:, 16:20], stat[:, 16:20], 60.0)
                inc(_li, "v")
            BV(_top2)
            mt2 = C["v"]

            def _sig(s, mt2=mt2):
                w(s, "v", mt2)
                _li = s.activation(stat[:, 16:20], stat[:, 16:20], ACT.Exp)
                inc(_li, "s")
            BS(_sig)
            msg_ = C["s"]

            def _wf(v, msg_=msg_):
                w(v, "s", msg_)
                _ld = v.tensor_scalar_add(stat[:, 16:20], stat[:, 16:20], 1.0)
                inc(_ld, "v"); w(v, "v")
                v.reciprocal(stat[:, 20:22], stat[:, 18:20])
                _ld = v.reciprocal(stat[:, 22:24], stat[:, 16:18])
                inc(_ld, "v"); w(v, "v")
                for tt in range(2):
                    sl = ltm_s[:, tt * 8:(tt + 1) * 8]
                    _ld = v.tensor_scalar(l8tmp[:, :], sl, stat[:, 12 + tt:13 + tt], None,
                                    ALU.is_equal)
                    inc(_ld, "v"); w(v, "v")
                    _ld = v.tensor_scalar_mul(wftm[:, tt * 8:(tt + 1) * 8], l8tmp[:, :],
                                        stat[:, 20 + tt:21 + tt])
                    inc(_ld, "v"); w(v, "v")
                    _ld = v.tensor_scalar(l8tmp[:, :], sl, stat[:, 14 + tt:15 + tt], None,
                                    ALU.is_equal)
                    inc(_ld, "v"); w(v, "v")
                    _ld = v.scalar_tensor_tensor(wftm[:, tt * 8:(tt + 1) * 8], l8tmp[:, :],
                                           stat[:, 22 + tt:23 + tt],
                                           wftm[:, tt * 8:(tt + 1) * 8],
                                           ALU.mult, ALU.add)
                    inc(_ld, "v"); w(v, "v")
                inc(v.drain(), "v")
            BV(_wf)
            mwf = C["v"]

            def _tr_wf(t, mwf=mwf):
                w(t, "v", mwf)
                for tt in range(2):
                    ti = t.transpose(b5[0:8, 256 + tt * 128:256 + (tt + 1) * 128],
                                     wftm[:, tt * 8:(tt + 1) * 8], ident[:, :])
                inc(ti, "t")
            block.tensor(_tr_wf)
            mtw = C["t"]

            def _ev_wf(v, mtw=mtw):
                w(v, "t", mtw)
                inc(v.tensor_copy(wfT[:, :], b5[0:8, 256:512]), "v")
            BV(_ev_wf)
            mewf = C["v"]

            def _wselmm(t, mewf=mewf):
                w(t, "v", mewf)
                ti = t.matmul(b4[:, 256:512], repmat[:, :], wfT[:, :],
                              start=True, stop=True)
                inc(ti, "t")
            block.tensor(_wselmm)
            mws = C["t"]

            def _ev_ws(v, mws=mws):
                w(v, "t", mws)
                inc(v.tensor_copy(wsel_s[:, :], b4[:, 256:512]), "v")
            BV(_ev_ws)
            mews = C["v"]

            def _lowmm(t, mews=mews):
                w(t, "v", mews)
                first = True
                for pi, (sta, mov) in enumerate(((a2th, ln2h), (a2th, ln2l), (a2tl, ln2h))):
                    for k in range(LC):
                        ti = t.matmul(b4[:, 0:256], sta[:, k * 128:(k + 1) * 128],
                                      mov[:, k * 256:(k + 1) * 256],
                                      start=first, stop=(pi == 2 and k == LC - 1))
                        first = False
                inc(ti, "t")
            block.tensor(_lowmm)
            mlow = C["t"]

            def _scaled(v, mlow=mlow):
                w(v, "t", mlow)
                _ld = v.tensor_tensor(scf[:, :], b4[:, 0:256], wsel_s[:, :], ALU.mult)
                inc(_ld, "v"); w(v, "v")
                _ld = v.tensor_copy(sch[:, :], scf[:, :])
                inc(_ld, "v"); w(v, "v")
                _li = v.tensor_tensor(scl[:, :], scf[:, :], sch[:, :], ALU.subtract)
                inc(_li, "v")
            BV(_scaled)
            b4_free = C["v"]
            msc = C["v"]

            for oc in range(NOC):
                bks = [2 * (oc % 2), 2 * (oc % 2) + 1]

                def _close(t, oc=oc, bks=bks, msc=msc):
                    w(t, "v", msc)
                    for tt in range(2):
                        t.matmul(bank[bks[tt]][:, :], sch[:, tt * 128:(tt + 1) * 128],
                                 b2h[:, oc * 512:(oc + 1) * 512], start=False, stop=False)
                        t.matmul(bank[bks[tt]][:, :], sch[:, tt * 128:(tt + 1) * 128],
                                 b2l[:, oc * 512:(oc + 1) * 512], start=False, stop=False)
                        ti = t.matmul(bank[bks[tt]][:, :], scl[:, tt * 128:(tt + 1) * 128],
                                      b2h[:, oc * 512:(oc + 1) * 512], start=False,
                                      stop=True)
                    inc(ti, "t")
                block.tensor(_close)
                mcl = C["t"]

                def _evac(v, oc=oc, bks=bks, mcl=mcl):
                    w(v, "t", mcl)
                    for tt in range(2):
                        _li = v.scalar_tensor_tensor(
                            res[:, tt * D + oc * 512: tt * D + (oc + 1) * 512],
                            bank[bks[tt]][:, :], 1.0, bias_b[:, oc * 512:(oc + 1) * 512],
                            ALU.mult, ALU.add,
                            accum_out=rsum_s[:, tt * 8 + oc: tt * 8 + oc + 1])
                    inc(_li, "v")
                BV(_evac)
                bank_free[bks[0]] = bank_free[bks[1]] = C["v"]
                mev = C["v"]

                def _sq(s, oc=oc, mev=mev):
                    w(s, "v", mev)
                    for tt in range(2):
                        _ld = s.activation(junk[:, 0:512],
                                     res[:, tt * D + oc * 512: tt * D + (oc + 1) * 512],
                                     ACT.Square,
                                     accum_out=rssq_s[:, tt * 8 + oc: tt * 8 + oc + 1])
                        inc(_ld, "s"); w(s, "s")
                    inc(s.drain(), "s")
                BS(_sq)

                if oc + 2 < NOC:
                    emit_mains(oc + 2)

            msqs = C["s"]

            def _ep_v(v, msqs=msqs):
                w(v, "s", msqs)
                v.drain()
                v.tensor_reduce(stat[:, 0:1], rsum_s[:, 0:8], AX, ALU.add)
                v.tensor_reduce(stat[:, 1:2], rsum_s[:, 8:16], AX, ALU.add)
                v.tensor_reduce(stat[:, 2:3], rssq_s[:, 0:8], AX, ALU.add)
                _ld = v.tensor_reduce(stat[:, 3:4], rssq_s[:, 8:16], AX, ALU.add)
                inc(_ld, "v"); w(v, "v")
                _ld = v.tensor_scalar_mul(stat[:, 4:6], stat[:, 0:2], INV_D)
                inc(_ld, "v"); w(v, "v")
                v.tensor_scalar_mul(stat[:, 8:10], stat[:, 2:4], INV_D)
                _ld = v.tensor_tensor(stat[:, 10:12], stat[:, 4:6], stat[:, 4:6], ALU.mult)
                inc(_ld, "v"); w(v, "v")
                _li = v.tensor_tensor(stat[:, 8:10], stat[:, 8:10], stat[:, 10:12],
                                ALU.subtract)
                inc(_li, "v")
            BV(_ep_v)
            mev2 = C["v"]

            def _ep_s(s, mev2=mev2):
                w(s, "v", mev2)
                _li = s.activation(stat[:, 10:12], stat[:, 8:10], ACT.Sqrt, bias=eps_s[:, :])
                inc(_li, "s")
            BS(_ep_s)
            mes = C["s"]

            def _ep_v2(v, mes=mes):
                w(v, "s", mes)
                _ld = v.reciprocal(stat[:, 6:8], stat[:, 10:12])
                inc(_ld, "v"); w(v, "v")
                for tt in range(2):
                    _ld = v.tensor_scalar(res[:, tt * D:(tt + 1) * D],
                                    res[:, tt * D:(tt + 1) * D],
                                    stat[:, 4 + tt:5 + tt], stat[:, 6 + tt:7 + tt],
                                    ALU.subtract, ALU.mult)
                inc(_ld, "v"); w(v, "v")
                for tt in range(2):
                    _li = v.tensor_tensor(x1[:, tt * D:(tt + 1) * D],
                                    x1[:, tt * D:(tt + 1) * D],
                                    res[:, tt * D:(tt + 1) * D], ALU.add)
                inc(_li, "v")
            BV(_ep_v2)
            mout = C["v"]

            def _out(g, n=n, mout=mout):
                w(g, "v", mout)
                inc(g.dma_start(out=y_d[n * 128:(n + 1) * 128, :],
                                in_=x1[:, 0:D]), "qo", 16)
                inc(g.dma_start(out=y_d[(8 + n) * 128:(9 + n) * 128, :],
                                in_=x1[:, D:2 * D]), "qo", 16)
            BG(_out)
            x_consumed_v = C["v"]
            x_consumed_s = C["s"]
            out_done = C["qo"]

        def _fin(sync):
            w(sync, "qo")
        block.sync(_fin)

    return nc


def kernel(L0_attn_x, L0_ffn_gate_w, L0_ffn_base_w, L0_ffn_base_b,
           L0_ffn_lora_A, L0_ffn_lora_B):
    x = np.ascontiguousarray(np.asarray(L0_attn_x, dtype=np.float32))
    gw = np.ascontiguousarray(np.asarray(L0_ffn_gate_w, dtype=np.float32))
    bw = np.asarray(L0_ffn_base_w, dtype=np.float32)
    bb = np.ascontiguousarray(np.asarray(L0_ffn_base_b, dtype=np.float32))
    lA = np.asarray(L0_ffn_lora_A, dtype=np.float32)
    lB = np.asarray(L0_ffn_lora_B, dtype=np.float32)

    wT = np.ascontiguousarray(bw.T)
    a2t = np.ascontiguousarray(lA.transpose(2, 0, 1).reshape(D, 128))
    b2 = np.ascontiguousarray(lB.transpose(0, 2, 1).reshape(128, D))

    nc = build_nc()
    in_maps = []
    for i in range(NCORES):
        xc = np.ascontiguousarray(x[:, i * CSL:(i + 1) * CSL, :].reshape(T, D))
        in_maps.append({"x": xc, "wT": wT, "gw": gw, "bias": bb,
                        "a2t": a2t, "b2": b2})
    resu = run_bass_kernel_spmd(nc, in_maps, core_ids=list(range(NCORES)))
    out = np.empty((B, SEQ, D), dtype=np.float32)
    for i in range(NCORES):
        out[:, i * CSL:(i + 1) * CSL, :] = np.asarray(
            resu.results[i]["y"]).reshape(B, CSL, D)
    return out


# version 09 (final)
# speedup vs baseline: 1.0110x; 1.0095x over previous
import numpy as np
import concourse.bass as bass
import concourse.mybir as mybir
from concourse.bass_utils import run_bass_kernel_spmd

F32 = mybir.dt.float32
BF16 = mybir.dt.bfloat16
AX = mybir.AxisListType.X
ALU = mybir.AluOpType
ACT = mybir.ActivationFunctionType

NCORES = 8
D = 4096
B = 2
SEQ = 8192
CSL = SEQ // NCORES
T = B * CSL
NT = T // 128
NPAIR = 8
LC = D // 128
NOC = 8
NCH = 8
SCALE = float(SEQ) ** 0.5
INV_D = 1.0 / D


def _addr(nc, handle):
    for a in nc.m.functions[0].allocations:
        if isinstance(a, mybir.MemoryLocationSet) and a.memorylocations:
            m = a.memorylocations[0]
            if m.name == handle.name:
                return m.addr
    raise KeyError(handle.name)


def build_nc():
    nc = bass.Bass("TRN2", target_bir_lowering=False)

    x_d = nc.dram_tensor("x", [T, D], F32, kind="ExternalInput")
    wT_d = nc.dram_tensor("wT", [D, D], F32, kind="ExternalInput")
    gw_d = nc.dram_tensor("gw", [8, D], F32, kind="ExternalInput")
    bias_d = nc.dram_tensor("bias", [D], F32, kind="ExternalInput")
    a2t_d = nc.dram_tensor("a2t", [D, 128], F32, kind="ExternalInput")
    b2_d = nc.dram_tensor("b2", [128, D], F32, kind="ExternalInput")
    y_d = nc.dram_tensor("y", [T, D], F32, kind="ExternalOutput")
    ar_in_d = nc.dram_tensor("ar_in", [128, 96], F32)
    ar_out_d = nc.dram_tensor("ar_out", [128, 96], F32)
    wh_scr = nc.dram_tensor("wh_scr", [NOC, 4, 128, 4096], BF16)
    wl_scr = nc.dram_tensor("wl_scr", [NOC, 4, 128, 4096], BF16)

    A = nc.alloc_sbuf_tensor
    x1 = A("x1", [128, 2 * D], F32)
    fm = A("fm", [128, 2 * D], F32)
    res = A("res", [128, 2 * D], F32)
    wbuf = A("wbuf", [128, 2 * D], F32)
    lnt = A("lnt", [128, D], F32)
    a2th = A("a2th", [128, D], BF16)
    a2tl = A("a2tl", [128, D], BF16)
    b2h = A("b2h", [128, D], BF16)
    b2l = A("b2l", [128, D], BF16)
    bias_b = A("bias_b", [128, D], F32)
    gwh = A("gwh", [128, 256], BF16)
    gwl = A("gwl", [128, 256], BF16)
    ident = A("ident", [128, 128], F32)
    repmat = A("repmat", [8, 128], F32)
    coliota = A("coliota", [128, 128], F32)
    rowiota = A("rowiota", [128, 1], F32)
    colio8 = A("colio8", [8, 128], F32)
    eps_s = A("eps_s", [128, 1], F32)
    mu1 = A("mu1", [128, NT], F32)
    rs1 = A("rs1", [128, NT], F32)
    partials = A("partials", [128, 96 * 8], F32)
    scores_g = A("scores_g", [128, 96], F32)
    p01 = A("p01", [128, LC], F32)
    p10 = A("p10", [128, LC], F32)
    p00p1 = A("p00p1", [128, LC], F32)
    p11p1 = A("p11p1", [128, LC], F32)
    dtmp = A("dtmp", [128, 128], F32)
    stat = A("stat", [128, 24], F32)
    rsum_s = A("rsum_s", [128, 16], F32)
    rssq_s = A("rssq_s", [128, 16], F32)
    ltm_s = A("ltm_s", [128, 16], F32)
    l8tmp = A("l8tmp", [128, 8], F32)
    wftm = A("wftm", [128, 16], F32)
    sch = A("sch", [128, 256], BF16)
    scl = A("scl", [128, 256], BF16)

    res_a = _addr(nc, res)
    fm_a = _addr(nc, fm)
    lnt_a = _addr(nc, lnt)
    wbuf_a = _addr(nc, wbuf)
    AT = nc.alloc_sbuf_tensor_at
    aot = AT("aot", [128, D], F32, offset=res_a)
    aof = AT("aof", [128, 512], F32, offset=res_a)
    aog = AT("aog", [128, D], F32, offset=res_a + 4 * D)
    st_h = AT("st_h", [128, 4096], BF16, offset=res_a)
    st_l = AT("st_l", [128, 4096], BF16, offset=res_a + 8192)
    wfp = AT("wfp", [128, 4096], F32, offset=res_a + 16384)
    b2f = AT("b2f", [128, D], F32, offset=fm_a)
    a2f = AT("a2f", [128, D], F32, offset=fm_a + 4 * D)
    gwnat = AT("gwnat", [8, D], F32, offset=lnt_a)
    ln2h = AT("ln2h", [128, 2 * D], BF16, offset=fm_a)
    ln2l = AT("ln2l", [128, 2 * D], BF16, offset=fm_a + 4 * D)
    le_s = AT("le_s", [8, 256], F32, offset=lnt_a + 2048)
    wfT = AT("wfT", [8, 256], F32, offset=lnt_a + 3072)
    wsel_s = AT("wsel_s", [128, 256], F32, offset=lnt_a + 4096)
    scf = AT("scf", [128, 256], F32, offset=lnt_a + 5120)
    wbh = AT("wbh", [128, 2 * D], BF16, offset=wbuf_a)
    wbl = AT("wbl", [128, 2 * D], BF16, offset=wbuf_a + 4 * D)
    junk = lnt

    P = nc.alloc_psum_tensor
    bank = [P(f"bk{i}", [128, 512], F32) for i in range(4)]
    b4 = P("b4", [128, 512], F32)
    b5 = P("b5", [128, 512], F32)
    b67 = [P("b6", [128, 512], F32), P("b7", [128, 512], F32)]

    sem = {k: nc.alloc_semaphore(name="sm_" + k)
           for k in ["qx", "qw0", "qw1", "qp0", "qp1", "qpw", "qg", "qo",
                     "v", "s", "t", "g", "cc", "kb"]}
    C = {k: 0 for k in sem}

    def inc(inst, k, n=1):
        inst.then_inc(sem[k], n)
        C[k] += n
        return C[k]

    def w(eng, k, val=None):
        eng.wait_ge(sem[k], C[k] if val is None else val)

    marks = {}

    def mark(name, k):
        marks[name] = C[k]

    ts = bass.AP

    with nc.Block() as block:
        def BV(f):
            block.vector(lambda v: (w(v, "v"), f(v)) and None)

        def BS(f):
            block.scalar(lambda s: (w(s, "s"), f(s)) and None)

        def BG(f):
            block.gpsimd(lambda g: (w(g, "g"), f(g)) and None)

        def _setup_sync(sync):
            inc(sync.dma_start(out=gwnat[:, :], in_=gw_d[:, :]), "qx", 16)
            inc(sync.dma_start(out=bias_b[:, :],
                               in_=bias_d.ap().partition_broadcast(128)), "qx", 16)
            inc(sync.dma_start(out=a2f[:, :],
                               in_=ts(a2t_d, 0, [[128, 128], [16384, 32], [1, 128]])), "qx", 16)
            inc(sync.dma_start(out=b2f[:, :], in_=b2_d[:, :]), "qx", 16)
        block.sync(_setup_sync)

        def _setup_g(g):
            g.iota(coliota[:, :], [[1, 128]], channel_multiplier=0,
                   allow_small_or_imprecise_dtypes=True)
            g.iota(rowiota[:, :], [[0, 1]], channel_multiplier=1,
                   allow_small_or_imprecise_dtypes=True)
            inc(g.iota(colio8[:, :], [[1, 128]], channel_multiplier=-16,
                       allow_small_or_imprecise_dtypes=True), "g")
        BG(_setup_g)

        def _setup_v(v):
            w(v, "qx")
            w(v, "g")
            v.memset(eps_s[:, :], 1e-5)
            v.tensor_scalar(ident[:, :], coliota[:, :], rowiota[:, :], None, ALU.is_equal)
            _ld = v.tensor_scalar(repmat[:, :], colio8[:, :], -0.5, None, ALU.is_ge)
            inc(_ld, "v"); w(v, "v")
            v.scalar_tensor_tensor(repmat[:, :], colio8[:, :], 15.5, repmat[:, :],
                                   ALU.is_le, ALU.mult)
            _ld = v.tensor_scalar_mul(b2h[:, :], b2f[:, :], 2.0)
            inc(_ld, "v"); w(v, "v")
            v.scalar_tensor_tensor(b2l[:, :], b2f[:, :], 2.0, b2h[:, :],
                                   ALU.mult, ALU.subtract)
            _ld = v.tensor_copy(a2th[:, :], a2f[:, :])
            inc(_ld, "v"); w(v, "v")
            _li = v.tensor_tensor(a2tl[:, :], a2f[:, :], a2th[:, :], ALU.subtract)
            inc(_li, "v")
        BV(_setup_v)
        mark("setup_v", "v")

        def _setup_t(t):
            w(t, "v", marks["setup_v"])
            for k in range(LC):
                ti = t.transpose(b5[:, k * 8:(k + 1) * 8],
                                 gwnat[0:8, k * 128:(k + 1) * 128], ident[0:8, 0:8])
            inc(ti, "t")
        block.tensor(_setup_t)
        mark("setup_t", "t")

        def _setup_v2(v):
            w(v, "t", marks["setup_t"])
            _ld = v.tensor_copy(gwh[:, :], b5[:, 0:256])
            inc(_ld, "v"); w(v, "v")
            _li = v.tensor_tensor(gwl[:, :], b5[:, 0:256], gwh[:, :], ALU.subtract)
            inc(_li, "v")
        BV(_setup_v2)
        mark("gw_done", "v")

        def ln_stats(src_ap, mu_ap, rs_ap, c):
            twait = C["t"]

            def _v1(v):
                v.drain()
                _li = v.tensor_reduce(stat[:, c:c + 1], src_ap, AX, ALU.add)
                inc(_li, "v")
            BV(_v1)
            mv = C["v"]

            def _s1(s):
                w(s, "v", mv)
                w(s, "t", twait)
                _li = s.activation(junk[:, 0:D], src_ap, ACT.Square,
                             accum_out=stat[:, c + 1:c + 2])
                inc(_li, "s")
            BS(_s1)
            msq = C["s"]

            def _v2(v):
                w(v, "s", msq)
                v.tensor_scalar_mul(mu_ap, stat[:, c:c + 1], INV_D)
                _ld = v.tensor_scalar_mul(stat[:, c + 2:c + 3], stat[:, c + 1:c + 2], INV_D)
                inc(_ld, "v"); w(v, "v")
                _ld = v.tensor_tensor(stat[:, c + 3:c + 4], mu_ap, mu_ap, ALU.mult)
                inc(_ld, "v"); w(v, "v")
                _li = v.tensor_tensor(stat[:, c + 2:c + 3], stat[:, c + 2:c + 3],
                                stat[:, c + 3:c + 4], ALU.subtract)
                inc(_li, "v")
            BV(_v2)
            mvar = C["v"]

            def _s2(s):
                w(s, "v", mvar)
                _li = s.activation(stat[:, c + 3:c + 4], stat[:, c + 2:c + 3], ACT.Ln,
                             bias=eps_s[:, :])
                inc(_li, "s")
                w(s, "s")
                _li = s.activation(stat[:, c + 3:c + 4], stat[:, c + 3:c + 4],
                                   ACT.Exp, scale=0.5)
                inc(_li, "s")
            BS(_s2)
            msd = C["s"]

            def _v3(v):
                w(v, "s", msd)
                _li = v.reciprocal(rs_ap, stat[:, c + 3:c + 4])
                inc(_li, "v")
            BV(_v3)

        allbk = bank + [b4, b5] + b67
        tr_state = {"banks_free": marks["gw_done"]}

        def transpose_tile_full(src_tensor, apply_mark, banks_wait):
            def _t(t):
                w(t, "v", apply_mark)
                w(t, "v", banks_wait)
                for lc in range(LC):
                    ti = t.transpose(allbk[lc // 4][:, (lc % 4) * 128:(lc % 4 + 1) * 128],
                                     src_tensor[:, lc * 128:(lc + 1) * 128],
                                     ident[:, :])
                inc(ti, "t")
            block.tensor(_t)
            return C["t"]

        def transpose_tile_to_fm(src_tensor, col_of_lc, apply_mark):
            mt = transpose_tile_full(src_tensor, apply_mark, tr_state["banks_free"])

            def _v(v, mt=mt):
                w(v, "t", mt)
                for bki in range(8):
                    lc0 = bki * 4
                    dst = ts(fm, col_of_lc(lc0), [[2 * D, 128], [256, 4], [1, 128]])
                    _ld = v.tensor_copy(dst, allbk[bki][:, :])
                inc(_ld, "v")
            BV(_v)
            tr_state["banks_free"] = C["v"]

        x_consumed_v = C["v"]
        x_consumed_s = C["s"]
        prep_rounds = [(oc, kb) for oc in range(NOC) for kb in range(4)]
        prep_i = 0
        marks["prep_wr"] = 0

        def emit_prep_round(oc, kb):
            def _ld(sync, oc=oc, kb=kb):
                sync.wait_ge(sem["g"], marks.get("prep_g", 0))
                src = ts(wT_d, (kb * 8 * 128) * D + oc * 512,
                         [[D, 128], [128 * D, 8], [1, 512]])
                inc(sync.dma_start(out=wfp[:, :], in_=src), "qp0", 16)
            block.sync(_ld)
            mq = C["qp0"]

            def _g(g, oc=oc, kb=kb, mq=mq):
                w(g, "qp0", mq)
                w(g, "qpw", marks.get("prep_wr", 0))
                _ld = g.tensor_copy(st_h[:, :], wfp[:, :])
                inc(_ld, "g"); w(g, "g")
                _li = g.tensor_tensor(st_l[:, :], wfp[:, :], st_h[:, :], ALU.subtract)
                inc(_li, "g")
                w(g, "g")
                g.dma_start(out=wh_scr[oc, kb], in_=st_h[:, :]).then_inc(sem["qpw"], 16)
                g.dma_start(out=wl_scr[oc, kb], in_=st_l[:, :]).then_inc(sem["qpw"], 16)
            BG(_g)
            C["qpw"] += 32
            marks["prep_g"] = C["g"]
            marks[f"scr_{oc}_{kb}"] = C["qpw"]
            marks["prep_wr"] = C["qpw"]

        def ln_stats_pair(src0, src1, twait=None):
            tw = C["t"] if twait is None else twait

            def _v1(v):
                v.drain()
                v.tensor_reduce(stat[:, 0:1], src0, AX, ALU.add)
                _li = v.tensor_reduce(stat[:, 1:2], src1, AX, ALU.add)
                inc(_li, "v")
            BV(_v1)
            mv = C["v"]

            def _s1(s, mv=mv, tw=tw):
                w(s, "v", mv)
                w(s, "t", tw)
                _ld = s.activation(junk[:, 0:D], src0, ACT.Square, accum_out=stat[:, 2:3])
                inc(_ld, "s"); w(s, "s")
                _li = s.activation(junk[:, 0:D], src1, ACT.Square, accum_out=stat[:, 3:4])
                inc(_li, "s")
            BS(_s1)
            msq = C["s"]

            def _v2(v, msq=msq):
                w(v, "s", msq)
                _ld = v.tensor_scalar_mul(stat[:, 4:6], stat[:, 0:2], INV_D)
                inc(_ld, "v"); w(v, "v")
                v.tensor_scalar_mul(stat[:, 8:10], stat[:, 2:4], INV_D)
                _ld = v.tensor_tensor(stat[:, 10:12], stat[:, 4:6], stat[:, 4:6], ALU.mult)
                inc(_ld, "v"); w(v, "v")
                _li = v.tensor_tensor(stat[:, 8:10], stat[:, 8:10], stat[:, 10:12],
                                ALU.subtract)
                inc(_li, "v")
            BV(_v2)
            mvar = C["v"]

            def _s2(s, mvar=mvar):
                w(s, "v", mvar)
                _li = s.activation(stat[:, 10:12], stat[:, 8:10], ACT.Ln,
                                   bias=eps_s[:, :])
                inc(_li, "s")
                w(s, "s")
                _li = s.activation(stat[:, 10:12], stat[:, 10:12], ACT.Exp, scale=0.5)
                inc(_li, "s")
            BS(_s2)
            msd = C["s"]

            def _v3(v, msd=msd):
                w(v, "s", msd)
                _li = v.reciprocal(stat[:, 6:8], stat[:, 10:12])
                inc(_li, "v")
            BV(_v3)

        junk1 = wbuf

        mx_next = None
        for p in range(NPAIR):
            if p == 0:
                xcv, xcs = x_consumed_v, x_consumed_s

                def _ldx(sync, p=p, xcv=xcv, xcs=xcs):
                    w(sync, "v", xcv)
                    w(sync, "s", xcs)
                    inc(sync.dma_start(out=x1[:, 0:D],
                                       in_=x_d[p * 128:(p + 1) * 128, :]), "qx", 16)
                    inc(sync.dma_start(out=x1[:, D:2 * D],
                                       in_=x_d[(8 + p) * 128:(9 + p) * 128, :]), "qx", 16)
                block.sync(_ldx)
                mx_next = C["qx"]
            mx = mx_next

            def _wx(v, mx=mx):
                w(v, "qx", mx)
                inc(v.drain(), "v")
            BV(_wx)

            def _wxs(s, mx=mx):
                w(s, "qx", mx)
                inc(s.drain(), "s")
            BS(_wxs)

            tw = C["t"]
            src0 = x1[:, 0:D]
            src1 = x1[:, D:2 * D]
            def _v1(v, src0=src0, src1=src1):
                v.drain()
                v.tensor_reduce(stat[:, 0:1], src0, AX, ALU.add)
                _li = v.tensor_reduce(stat[:, 1:2], src1, AX, ALU.add)
                inc(_li, "v")
            BV(_v1)
            mv = C["v"]

            def _s1(v, mv=mv, src0=src0, src1=src1):
                v.scalar_tensor_tensor(junk1[:, 0:D], src0, 1.0, src0,
                                       ALU.mult, ALU.mult, accum_out=stat[:, 2:3])
                _li = v.scalar_tensor_tensor(junk1[:, D:2 * D], src1, 1.0, src1,
                                             ALU.mult, ALU.mult,
                                             accum_out=stat[:, 3:4])
                inc(_li, "v")
            BV(_s1)
            msq = C["v"]

            def _v2(v, p=p, msq=msq):
                w(v, "v", msq)
                _ld = v.tensor_scalar_mul(stat[:, 4:6], stat[:, 0:2], INV_D)
                inc(_ld, "v"); w(v, "v")
                v.tensor_scalar_mul(stat[:, 8:10], stat[:, 2:4], INV_D)
                _ld = v.tensor_tensor(stat[:, 10:12], stat[:, 4:6], stat[:, 4:6], ALU.mult)
                inc(_ld, "v"); w(v, "v")
                _li = v.tensor_tensor(stat[:, 8:10], stat[:, 8:10], stat[:, 10:12],
                                ALU.subtract)
                inc(_li, "v")
            BV(_v2)
            mvar = C["v"]

            def _s2(s, mvar=mvar):
                w(s, "v", mvar)
                _li = s.activation(stat[:, 10:12], stat[:, 8:10], ACT.Ln,
                                   bias=eps_s[:, :])
                inc(_li, "s")
                w(s, "s")
                _li = s.activation(stat[:, 10:12], stat[:, 10:12], ACT.Exp, scale=0.5)
                inc(_li, "s")
            BS(_s2)
            msd = C["s"]

            def _v3(v, p=p, msd=msd):
                w(v, "s", msd)
                _ld = v.reciprocal(stat[:, 6:8], stat[:, 10:12])
                inc(_ld, "v"); w(v, "v")
                v.tensor_copy(mu1[:, p:p + 1], stat[:, 4:5])
                v.tensor_copy(mu1[:, 8 + p:9 + p], stat[:, 5:6])
                v.tensor_copy(rs1[:, p:p + 1], stat[:, 6:7])
                _li = v.tensor_copy(rs1[:, 8 + p:9 + p], stat[:, 7:8])
                inc(_li, "v")
            BV(_v3)

            for b in range(2):
                g = p + 8 * b
                src = x1[:, b * D:(b + 1) * D]

                def _apply(v, src=src, g=g):
                    _li = v.tensor_scalar(lnt[:, 0:D], src, mu1[:, g:g + 1], rs1[:, g:g + 1],
                                    ALU.subtract, ALU.mult)
                    inc(_li, "v")
                BV(_apply)
                transpose_tile_to_fm(lnt, lambda lc, b=b: lc * 256 + b * 128, C["v"])

            x_consumed_v = C["v"]
            x_consumed_s = C["s"]
            if p + 1 < NPAIR:
                xcv, xcs = x_consumed_v, x_consumed_s

                def _ldx2(sync, p=p, xcv=xcv, xcs=xcs):
                    w(sync, "v", xcv)
                    w(sync, "s", xcs)
                    inc(sync.dma_start(out=x1[:, 0:D],
                                       in_=x_d[(p + 1) * 128:(p + 2) * 128, :]), "qx", 16)
                    inc(sync.dma_start(out=x1[:, D:2 * D],
                                       in_=x_d[(9 + p) * 128:(10 + p) * 128, :]), "qx", 16)
                block.sync(_ldx2)
                mx_next = C["qx"]

            for _ in range(4):
                if prep_i < len(prep_rounds):
                    emit_prep_round(*prep_rounds[prep_i])
                    prep_i += 1

            mqfm = C["v"]

            def _sqp(s, p=p, mqfm=mqfm):
                w(s, "v", mqfm)
                q0 = ts(fm, 0, [[2 * D, 128], [256, 32], [1, 128]])
                q1 = ts(fm, 128, [[2 * D, 128], [256, 32], [1, 128]])
                s.drain()
                _ld = s.activation(junk1[:, 0:D], q0, ACT.Square)
                inc(_ld, "s"); w(s, "s")
                _li = s.activation(junk1[:, D:2 * D], q1, ACT.Square)
                inc(_li, "s")
            BS(_sqp)
            msq2 = C["s"]

            def _prod(v, p=p, msq2=msq2):
                q0 = ts(fm, 0, [[2 * D, 128], [256, 32], [1, 128]])
                q1 = ts(fm, 128, [[2 * D, 128], [256, 32], [1, 128]])
                v.drain()
                _ld = v.tensor_tensor(lnt[:, 0:D], q0, q1, ALU.mult)
                inc(_ld, "v"); w(v, "v")
                w(v, "s", msq2)
                for pair, (tsr, off) in ((0, (junk1, 0)), (1, (lnt, 0)),
                                         (2, (junk1, D))):
                    srcap = ts(tsr, off, [[tsr.shape[1], 128], [128, 32], [1, 128]])
                    dst = ts(partials, pair * 256 + p, [[96 * 8, 128], [8, 32]])
                    _li = v.tensor_reduce(dst, srcap, AX, ALU.add)
                inc(_li, "v")
            BV(_prod)
            marks["p1_prod_v"] = C["v"]
            marks["p1_sq_s"] = C["s"]

        while prep_i < len(prep_rounds):
            emit_prep_round(*prep_rounds[prep_i])
            prep_i += 1

        def _red(v):
            v.drain()
            _li = v.tensor_reduce(scores_g[:, :],
                            ts(partials, 0, [[96 * 8, 128], [8, 96], [1, 8]]),
                            AX, ALU.add)
            inc(_li, "v")
        BV(_red)
        mark("scores_local", "v")

        def _ar(g):
            w(g, "v", marks["scores_local"])
            inc(g.dma_start(out=ar_in_d[:, :], in_=scores_g[:, :]), "qg", 16)
            w(g, "qg")
            inc(g.collective_compute("AllReduce", ALU.add,
                                     replica_groups=[list(range(NCORES))],
                                     ins=[ar_in_d.ap().opt()],
                                     outs=[ar_out_d.ap().opt()]), "cc")
            w(g, "cc")
            inc(g.dma_start(out=scores_g[:, :], in_=ar_out_d[:, :]), "qg", 16)
        BG(_ar)
        mar = C["qg"]

        def _arows_v(v):
            w(v, "qg", mar)
            v.tensor_tensor(dtmp[:, 0:32], scores_g[:, 32:64], scores_g[:, 0:32],
                            ALU.subtract)
            _ld = v.tensor_tensor(dtmp[:, 32:64], scores_g[:, 32:64], scores_g[:, 64:96],
                            ALU.subtract)
            inc(_ld, "v"); w(v, "v")
            _ld = v.tensor_scalar_mul(dtmp[:, 0:64], dtmp[:, 0:64], 1.0 / SCALE)
            inc(_ld, "v"); w(v, "v")
            _ld = v.tensor_scalar_mul(dtmp[:, 64:128], dtmp[:, 0:64], -1.0)
            inc(_ld, "v"); w(v, "v")
            _li = v.tensor_scalar_min(dtmp[:, :], dtmp[:, :], 60.0)
            inc(_li, "v")
        BV(_arows_v)
        mdv = C["v"]

        def _arows_s(s):
            w(s, "v", mdv)
            _li = s.activation(dtmp[:, :], dtmp[:, :], ACT.Exp)
            inc(_li, "s")
        BS(_arows_s)
        mas = C["s"]

        def _arows_v2(v):
            w(v, "s", mas)
            _ld = v.tensor_scalar_add(dtmp[:, :], dtmp[:, :], 1.0)
            inc(_ld, "v"); w(v, "v")
            _ld = v.reciprocal(dtmp[:, :], dtmp[:, :])
            inc(_ld, "v"); w(v, "v")
            v.tensor_copy(p01[:, :], dtmp[:, 64:96])
            v.tensor_copy(p10[:, :], dtmp[:, 96:128])
            v.tensor_scalar_add(p00p1[:, :], dtmp[:, 0:32], 1.0)
            _li = v.tensor_scalar_add(p11p1[:, :], dtmp[:, 32:64], 1.0)
            inc(_li, "v")
        BV(_arows_v2)

        mprep_wr = marks["prep_wr"]
        mprep_g = marks.get("prep_g", 0)

        def _prep_fence(v, mprep_wr=mprep_wr, mprep_g=mprep_g):
            w(v, "qpw", mprep_wr)
            w(v, "g", mprep_g)
            inc(v.drain(), "v")
        BV(_prep_fence)
        mark("arows", "v")

        bank_free = [marks["arows"]] * 4
        b4_free = marks["arows"]
        wseq = 0
        out_done = 0

        for n in range(NCH):
            g0, g1 = n, 8 + n
            xcv, xcs, od = x_consumed_v, x_consumed_s, out_done

            def _ldx(sync, n=n, xcv=xcv, xcs=xcs, od=od):
                w(sync, "v", xcv)
                w(sync, "s", xcs)
                w(sync, "qo", od)
                inc(sync.dma_start(out=x1[:, 0:D], in_=x_d[n * 128:(n + 1) * 128, :]),
                    "qx", 16)
                inc(sync.dma_start(out=x1[:, D:2 * D],
                                   in_=x_d[(8 + n) * 128:(9 + n) * 128, :]), "qx", 16)
            block.sync(_ldx)
            mx = C["qx"]

            def _wpre(sc, n=n, wbase=wseq):
                if n == 0:
                    sc.wait_ge(sem["s"], marks["p1_sq_s"])
                    sc.wait_ge(sem["v"], marks["p1_prod_v"])
                for wv in range(32):
                    oc, kb = divmod(wv, 4)
                    seqi = wbase + wv
                    par = seqi % 2
                    if seqi >= 2:
                        sc.wait_ge(sem["kb"], seqi - 1)
                    if n == 0:
                        sc.wait_ge(sem["qpw"], marks[f"scr_{oc}_{kb}"])
                    inc(sc.dma_start(out=wbh[:, par * D:(par + 1) * D],
                                     in_=wh_scr[oc, kb]), f"qw{par}", 16)
                    inc(sc.dma_start(out=wbl[:, par * D:(par + 1) * D],
                                     in_=wl_scr[oc, kb]), f"qw{par}", 16)
            block.sync(_wpre)
            wseq += 32

            def _wx(v, mx=mx):
                w(v, "qx", mx)
                inc(v.drain(), "v")
            BV(_wx)

            def _wxs(s, mx=mx):
                w(s, "qx", mx)
                inc(s.drain(), "s")
            BS(_wxs)

            for b in range(2):
                g = (g0, g1)[b]

                def _apply1(v, b=b, g=g):
                    _li = v.tensor_scalar(lnt[:, 0:D], x1[:, b * D:(b + 1) * D],
                                    mu1[:, g:g + 1], rs1[:, g:g + 1],
                                    ALU.subtract, ALU.mult)
                    inc(_li, "v")
                BV(_apply1)
                transpose_tile_to_fm(lnt, lambda lc, b=b: lc * 256 + b * 128, C["v"])

            for b in range(2):
                pmul = p01 if b == 0 else p10
                pself = p00p1 if b == 0 else p11p1

                def _comb(v, b=b, pmul=pmul, pself=pself):
                    for grp in range(8):
                        for j in range(4):
                            lc = grp * 4 + j
                            qo = fm[:, lc * 256 + (1 - b) * 128:
                                    lc * 256 + (1 - b) * 128 + 128]
                            _ld = v.tensor_scalar_mul(aof[:, j * 128:(j + 1) * 128],
                                                      qo, pmul[:, lc:lc + 1])
                        inc(_ld, "v"); w(v, "v")
                        for j in range(4):
                            lc = grp * 4 + j
                            qs = fm[:, lc * 256 + b * 128: lc * 256 + b * 128 + 128]
                            _ld = v.scalar_tensor_tensor(
                                aog[:, lc * 128:(lc + 1) * 128],
                                qs, pself[:, lc:lc + 1], aof[:, j * 128:(j + 1) * 128],
                                ALU.mult, ALU.add)
                        inc(_ld, "v"); w(v, "v")
                    inc(v.drain(), "v")
                BV(_comb)
                mt = transpose_tile_full(aog, C["v"], tr_state["banks_free"])

                def _evb(v, mt=mt):
                    w(v, "t", mt)
                    for bki in range(8):
                        _ld = v.tensor_copy(aot[:, bki * 512:(bki + 1) * 512],
                                            allbk[bki][:, :])
                    inc(_ld, "v")
                BV(_evb)
                tr_state["banks_free"] = C["v"]

                def _as1(v):
                    v.tensor_reduce(stat[:, 0:1], aot[:, 0:D], AX, ALU.add)
                    _ld = v.scalar_tensor_tensor(aog[:, :], aot[:, 0:D], 1.0,
                                                 aot[:, 0:D], ALU.mult, ALU.mult,
                                                 accum_out=stat[:, 1:2])
                    inc(_ld, "v"); w(v, "v")
                    v.tensor_scalar_mul(stat[:, 4:5], stat[:, 0:1], INV_D)
                    _ld = v.tensor_scalar_mul(stat[:, 2:3], stat[:, 1:2], INV_D)
                    inc(_ld, "v"); w(v, "v")
                    _ld = v.tensor_tensor(stat[:, 3:4], stat[:, 4:5], stat[:, 4:5],
                                          ALU.mult)
                    inc(_ld, "v"); w(v, "v")
                    _li = v.tensor_tensor(stat[:, 2:3], stat[:, 2:3], stat[:, 3:4],
                                          ALU.subtract)
                    inc(_li, "v")
                BV(_as1)
                mv = C["v"]

                def _as2(s, mv=mv):
                    w(s, "v", mv)
                    _li = s.activation(stat[:, 3:4], stat[:, 2:3], ACT.Ln,
                                       bias=eps_s[:, :])
                    inc(_li, "s")
                    w(s, "s")
                    _li = s.activation(stat[:, 3:4], stat[:, 3:4], ACT.Exp, scale=0.5)
                    inc(_li, "s")
                BS(_as2)
                ms = C["s"]

                def _addres(v, b=b, ms=ms):
                    w(v, "s", ms)
                    _ld = v.reciprocal(stat[:, 5:6], stat[:, 3:4])
                    inc(_ld, "v"); w(v, "v")
                    _ld = v.tensor_scalar(aot[:, 0:D], aot[:, 0:D], stat[:, 4:5],
                                          stat[:, 5:6], ALU.subtract, ALU.mult)
                    inc(_ld, "v"); w(v, "v")
                    _li = v.tensor_tensor(x1[:, b * D:(b + 1) * D],
                                          x1[:, b * D:(b + 1) * D],
                                          aot[:, 0:D], ALU.add)
                    inc(_li, "v")
                BV(_addres)

            def _bs1(v):
                v.tensor_reduce(stat[:, 0:1], x1[:, 0:D], AX, ALU.add)
                v.tensor_reduce(stat[:, 1:2], x1[:, D:2 * D], AX, ALU.add)
                _ld = v.scalar_tensor_tensor(lnt[:, 0:D], x1[:, 0:D], 1.0, x1[:, 0:D],
                                             ALU.mult, ALU.mult,
                                             accum_out=stat[:, 2:3])
                inc(_ld, "v"); w(v, "v")
                _ld = v.scalar_tensor_tensor(lnt[:, 0:D], x1[:, D:2 * D], 1.0,
                                             x1[:, D:2 * D], ALU.mult, ALU.mult,
                                             accum_out=stat[:, 3:4])
                inc(_ld, "v"); w(v, "v")
                v.tensor_scalar_mul(stat[:, 4:6], stat[:, 0:2], INV_D)
                _ld = v.tensor_scalar_mul(stat[:, 8:10], stat[:, 2:4], INV_D)
                inc(_ld, "v"); w(v, "v")
                _ld = v.tensor_tensor(stat[:, 10:12], stat[:, 4:6], stat[:, 4:6],
                                      ALU.mult)
                inc(_ld, "v"); w(v, "v")
                _li = v.tensor_tensor(stat[:, 8:10], stat[:, 8:10], stat[:, 10:12],
                                      ALU.subtract)
                inc(_li, "v")
            BV(_bs1)
            mv = C["v"]

            def _bs2(s, mv=mv):
                w(s, "v", mv)
                _li = s.activation(stat[:, 10:12], stat[:, 8:10], ACT.Ln,
                                   bias=eps_s[:, :])
                inc(_li, "s")
                w(s, "s")
                _li = s.activation(stat[:, 10:12], stat[:, 10:12], ACT.Exp, scale=0.5)
                inc(_li, "s")
            BS(_bs2)
            ms = C["s"]

            def _bs3(v, ms=ms):
                w(v, "s", ms)
                _li = v.reciprocal(stat[:, 6:8], stat[:, 10:12])
                inc(_li, "v")
            BV(_bs3)

            for b in range(2):
                def _apply2(v, b=b):
                    _li = v.tensor_scalar(lnt[:, 0:D], x1[:, b * D:(b + 1) * D],
                                    stat[:, 4 + b:5 + b], stat[:, 6 + b:7 + b],
                                    ALU.subtract, ALU.mult)
                    inc(_li, "v")
                BV(_apply2)
                mt = transpose_tile_full(lnt, C["v"], tr_state["banks_free"])

                def _bhl(v, b=b, mt=mt):
                    w(v, "t", mt)
                    for bki in range(8):
                        lc0 = bki * 4
                        dsth = ts(ln2h, lc0 * 256 + b * 128,
                                  [[2 * D, 128], [256, 4], [1, 128]])
                        _ld = v.tensor_copy(dsth, allbk[bki][:, :])
                    inc(_ld, "v"); w(v, "v")
                    for bki in range(8):
                        lc0 = bki * 4
                        dsth = ts(ln2h, lc0 * 256 + b * 128,
                                  [[2 * D, 128], [256, 4], [1, 128]])
                        dstl = ts(ln2l, lc0 * 256 + b * 128,
                                  [[2 * D, 128], [256, 4], [1, 128]])
                        _ld = v.tensor_tensor(dstl, allbk[bki][:, :], dsth,
                                              ALU.subtract)
                    inc(_ld, "v")
                BV(_bhl)
                tr_state["banks_free"] = C["v"]

            mfm = C["v"]
            bf = b4_free

            def emit_mains(oc):
                bks = [2 * (oc % 2), 2 * (oc % 2) + 1]
                bfr = [bank_free[bks[0]], bank_free[bks[1]]]

                def _main(t, n=n, oc=oc, bks=bks, bfr=bfr, mfm=mfm):
                    w(t, "v", mfm)
                    w(t, "v", max(bfr))
                    for kb in range(4):
                        seqi = n * 32 + oc * 4 + kb
                        par = seqi % 2
                        t.wait_ge(sem[f"qw{par}"], 32 * (seqi // 2 + 1))
                        for k8 in range(8):
                            k = kb * 8 + k8
                            wh_sl = wbh[:, par * D + k8 * 512: par * D + (k8 + 1) * 512]
                            wl_sl = wbl[:, par * D + k8 * 512: par * D + (k8 + 1) * 512]
                            for tt in range(2):
                                h_sl = ln2h[:, k * 256 + tt * 128: k * 256 + tt * 128 + 128]
                                l_sl = ln2l[:, k * 256 + tt * 128: k * 256 + tt * 128 + 128]
                                t.matmul(bank[bks[tt]][:, :], h_sl, wh_sl,
                                         start=(kb == 0 and k8 == 0), stop=False)
                                t.matmul(bank[bks[tt]][:, :], h_sl, wl_sl,
                                         start=False, stop=False)
                                ti = t.matmul(bank[bks[tt]][:, :], l_sl, wh_sl,
                                              start=False, stop=False)
                        inc(ti, "kb")
                block.tensor(_main)

            def _gate(t, mfm=mfm, bf=bf):
                w(t, "v", mfm)
                w(t, "v", bf)
                first = True
                for pi, (sta, mov) in enumerate(((gwh, ln2h), (gwh, ln2l), (gwl, ln2h))):
                    for k in range(LC):
                        ti = t.matmul(b4[0:8, 0:256], sta[:, k * 8:(k + 1) * 8],
                                      mov[:, k * 256:(k + 1) * 256],
                                      start=first, stop=(pi == 2 and k == LC - 1))
                        first = False
                inc(ti, "t")
            block.tensor(_gate)
            mg = C["t"]

            emit_mains(0)

            def _ev_le(v, mg=mg):
                w(v, "t", mg)
                inc(v.tensor_copy(le_s[:, :], b4[0:8, 0:256]), "v")
            BV(_ev_le)
            mle = C["v"]

            def _tr_l(t, mle=mle):
                w(t, "v", mle)
                for tt in range(2):
                    ti = t.transpose(b5[:, tt * 8:(tt + 1) * 8],
                                     le_s[0:8, tt * 128:(tt + 1) * 128], ident[0:8, 0:8])
                inc(ti, "t")
            block.tensor(_tr_l)
            mtl = C["t"]

            emit_mains(1)

            def _top2(v, mtl=mtl):
                w(v, "t", mtl)
                _ld = v.tensor_copy(ltm_s[:, :], b5[:, 0:16])
                inc(_ld, "v"); w(v, "v")
                for tt in range(2):
                    sl = ltm_s[:, tt * 8:(tt + 1) * 8]
                    _ld = v.max(out=l8tmp[:, :], in_=sl)
                    inc(_ld, "v"); w(v, "v")
                    v.tensor_copy(stat[:, 12 + tt:13 + tt], l8tmp[:, 0:1])
                    _ld = v.tensor_copy(stat[:, 14 + tt:15 + tt], l8tmp[:, 1:2])
                    inc(_ld, "v"); w(v, "v")
                _ld = v.tensor_tensor(stat[:, 16:18], stat[:, 12:14], stat[:, 14:16],
                                ALU.subtract)
                inc(_ld, "v"); w(v, "v")
                _ld = v.tensor_scalar_mul(stat[:, 18:20], stat[:, 16:18], -1.0)
                inc(_ld, "v"); w(v, "v")
                _li = v.tensor_scalar_min(stat[:, 16:20], stat[:, 16:20], 60.0)
                inc(_li, "v")
            BV(_top2)
            mt2 = C["v"]

            def _sig(s, mt2=mt2):
                w(s, "v", mt2)
                _li = s.activation(stat[:, 16:20], stat[:, 16:20], ACT.Exp)
                inc(_li, "s")
            BS(_sig)
            msg_ = C["s"]

            def _wf(v, msg_=msg_):
                w(v, "s", msg_)
                _ld = v.tensor_scalar_add(stat[:, 16:20], stat[:, 16:20], 1.0)
                inc(_ld, "v"); w(v, "v")
                v.reciprocal(stat[:, 20:22], stat[:, 18:20])
                _ld = v.reciprocal(stat[:, 22:24], stat[:, 16:18])
                inc(_ld, "v"); w(v, "v")
                for tt in range(2):
                    sl = ltm_s[:, tt * 8:(tt + 1) * 8]
                    _ld = v.tensor_scalar(l8tmp[:, :], sl, stat[:, 12 + tt:13 + tt], None,
                                    ALU.is_equal)
                    inc(_ld, "v"); w(v, "v")
                    _ld = v.tensor_scalar_mul(wftm[:, tt * 8:(tt + 1) * 8], l8tmp[:, :],
                                        stat[:, 20 + tt:21 + tt])
                    inc(_ld, "v"); w(v, "v")
                    _ld = v.tensor_scalar(l8tmp[:, :], sl, stat[:, 14 + tt:15 + tt], None,
                                    ALU.is_equal)
                    inc(_ld, "v"); w(v, "v")
                    _ld = v.scalar_tensor_tensor(wftm[:, tt * 8:(tt + 1) * 8], l8tmp[:, :],
                                           stat[:, 22 + tt:23 + tt],
                                           wftm[:, tt * 8:(tt + 1) * 8],
                                           ALU.mult, ALU.add)
                    inc(_ld, "v"); w(v, "v")
                inc(v.drain(), "v")
            BV(_wf)
            mwf = C["v"]

            def _tr_wf(t, mwf=mwf):
                w(t, "v", mwf)
                for tt in range(2):
                    ti = t.transpose(b5[0:8, 256 + tt * 128:256 + (tt + 1) * 128],
                                     wftm[:, tt * 8:(tt + 1) * 8], ident[:, :])
                inc(ti, "t")
            block.tensor(_tr_wf)
            mtw = C["t"]

            def _ev_wf(v, mtw=mtw):
                w(v, "t", mtw)
                inc(v.tensor_copy(wfT[:, :], b5[0:8, 256:512]), "v")
            BV(_ev_wf)
            mewf = C["v"]

            def _wselmm(t, mewf=mewf):
                w(t, "v", mewf)
                ti = t.matmul(b4[:, 256:512], repmat[:, :], wfT[:, :],
                              start=True, stop=True)
                inc(ti, "t")
            block.tensor(_wselmm)
            mws = C["t"]

            def _ev_ws(v, mws=mws):
                w(v, "t", mws)
                inc(v.tensor_copy(wsel_s[:, :], b4[:, 256:512]), "v")
            BV(_ev_ws)
            mews = C["v"]

            def _lowmm(t, mews=mews):
                w(t, "v", mews)
                first = True
                for pi, (sta, mov) in enumerate(((a2th, ln2h), (a2th, ln2l), (a2tl, ln2h))):
                    for k in range(LC):
                        ti = t.matmul(b4[:, 0:256], sta[:, k * 128:(k + 1) * 128],
                                      mov[:, k * 256:(k + 1) * 256],
                                      start=first, stop=(pi == 2 and k == LC - 1))
                        first = False
                inc(ti, "t")
            block.tensor(_lowmm)
            mlow = C["t"]

            def _scaled(v, mlow=mlow):
                w(v, "t", mlow)
                _ld = v.tensor_tensor(scf[:, :], b4[:, 0:256], wsel_s[:, :], ALU.mult)
                inc(_ld, "v"); w(v, "v")
                _ld = v.tensor_copy(sch[:, :], scf[:, :])
                inc(_ld, "v"); w(v, "v")
                _li = v.tensor_tensor(scl[:, :], scf[:, :], sch[:, :], ALU.subtract)
                inc(_li, "v")
            BV(_scaled)
            b4_free = C["v"]
            msc = C["v"]

            for oc in range(NOC):
                bks = [2 * (oc % 2), 2 * (oc % 2) + 1]

                def _close(t, oc=oc, bks=bks, msc=msc):
                    w(t, "v", msc)
                    for tt in range(2):
                        t.matmul(bank[bks[tt]][:, :], sch[:, tt * 128:(tt + 1) * 128],
                                 b2h[:, oc * 512:(oc + 1) * 512], start=False, stop=False)
                        t.matmul(bank[bks[tt]][:, :], sch[:, tt * 128:(tt + 1) * 128],
                                 b2l[:, oc * 512:(oc + 1) * 512], start=False, stop=False)
                        ti = t.matmul(bank[bks[tt]][:, :], scl[:, tt * 128:(tt + 1) * 128],
                                      b2h[:, oc * 512:(oc + 1) * 512], start=False,
                                      stop=True)
                    inc(ti, "t")
                block.tensor(_close)
                mcl = C["t"]

                def _evac(v, oc=oc, bks=bks, mcl=mcl):
                    w(v, "t", mcl)
                    for tt in range(2):
                        _li = v.scalar_tensor_tensor(
                            res[:, tt * D + oc * 512: tt * D + (oc + 1) * 512],
                            bank[bks[tt]][:, :], 1.0, bias_b[:, oc * 512:(oc + 1) * 512],
                            ALU.mult, ALU.add,
                            accum_out=rsum_s[:, tt * 8 + oc: tt * 8 + oc + 1])
                    inc(_li, "v")
                BV(_evac)
                bank_free[bks[0]] = bank_free[bks[1]] = C["v"]
                mev = C["v"]

                if oc + 2 < NOC:
                    emit_mains(oc + 2)

            mevall = C["v"]

            def _sqall(s, mevall=mevall):
                w(s, "v", mevall)
                for i, (tt, oc) in enumerate([(tt, oc) for tt in range(2)
                                              for oc in range(NOC)]):
                    if i == 8:
                        inc(_ld, "s")
                        w(s, "s")
                    _ld = s.activation(junk[:, (i % 8) * 512:(i % 8 + 1) * 512],
                                       res[:, tt * D + oc * 512: tt * D + (oc + 1) * 512],
                                       ACT.Square,
                                       accum_out=rssq_s[:, tt * 8 + oc: tt * 8 + oc + 1])
                inc(_ld, "s")
            BS(_sqall)

            msqs = C["s"]

            def _ep_v(v, msqs=msqs):
                w(v, "s", msqs)
                v.drain()
                v.tensor_reduce(stat[:, 0:1], rsum_s[:, 0:8], AX, ALU.add)
                v.tensor_reduce(stat[:, 1:2], rsum_s[:, 8:16], AX, ALU.add)
                v.tensor_reduce(stat[:, 2:3], rssq_s[:, 0:8], AX, ALU.add)
                _ld = v.tensor_reduce(stat[:, 3:4], rssq_s[:, 8:16], AX, ALU.add)
                inc(_ld, "v"); w(v, "v")
                _ld = v.tensor_scalar_mul(stat[:, 4:6], stat[:, 0:2], INV_D)
                inc(_ld, "v"); w(v, "v")
                v.tensor_scalar_mul(stat[:, 8:10], stat[:, 2:4], INV_D)
                _ld = v.tensor_tensor(stat[:, 10:12], stat[:, 4:6], stat[:, 4:6], ALU.mult)
                inc(_ld, "v"); w(v, "v")
                _li = v.tensor_tensor(stat[:, 8:10], stat[:, 8:10], stat[:, 10:12],
                                ALU.subtract)
                inc(_li, "v")
            BV(_ep_v)
            mev2 = C["v"]

            def _ep_s(s, mev2=mev2):
                w(s, "v", mev2)
                _li = s.activation(stat[:, 10:12], stat[:, 8:10], ACT.Ln,
                                   bias=eps_s[:, :])
                inc(_li, "s")
                w(s, "s")
                _li = s.activation(stat[:, 10:12], stat[:, 10:12], ACT.Exp, scale=0.5)
                inc(_li, "s")
            BS(_ep_s)
            mes = C["s"]

            def _ep_v2(v, mes=mes):
                w(v, "s", mes)
                _ld = v.reciprocal(stat[:, 6:8], stat[:, 10:12])
                inc(_ld, "v"); w(v, "v")
                for tt in range(2):
                    _ld = v.tensor_scalar(res[:, tt * D:(tt + 1) * D],
                                    res[:, tt * D:(tt + 1) * D],
                                    stat[:, 4 + tt:5 + tt], stat[:, 6 + tt:7 + tt],
                                    ALU.subtract, ALU.mult)
                inc(_ld, "v"); w(v, "v")
                for tt in range(2):
                    _li = v.tensor_tensor(x1[:, tt * D:(tt + 1) * D],
                                    x1[:, tt * D:(tt + 1) * D],
                                    res[:, tt * D:(tt + 1) * D], ALU.add)
                inc(_li, "v")
            BV(_ep_v2)
            mout = C["v"]

            def _out(g, n=n, mout=mout):
                w(g, "v", mout)
                inc(g.dma_start(out=y_d[n * 128:(n + 1) * 128, :],
                                in_=x1[:, 0:D]), "qo", 16)
                inc(g.dma_start(out=y_d[(8 + n) * 128:(9 + n) * 128, :],
                                in_=x1[:, D:2 * D]), "qo", 16)
            BG(_out)
            x_consumed_v = C["v"]
            x_consumed_s = C["s"]
            out_done = C["qo"]

        def _fin(sync):
            w(sync, "qo")
        block.sync(_fin)

    return nc


def kernel(L0_attn_x, L0_ffn_gate_w, L0_ffn_base_w, L0_ffn_base_b,
           L0_ffn_lora_A, L0_ffn_lora_B):
    x = np.ascontiguousarray(np.asarray(L0_attn_x, dtype=np.float32))
    gw = np.ascontiguousarray(np.asarray(L0_ffn_gate_w, dtype=np.float32))
    bw = np.asarray(L0_ffn_base_w, dtype=np.float32)
    bb = np.ascontiguousarray(np.asarray(L0_ffn_base_b, dtype=np.float32))
    lA = np.asarray(L0_ffn_lora_A, dtype=np.float32)
    lB = np.asarray(L0_ffn_lora_B, dtype=np.float32)

    wT = np.ascontiguousarray(bw.T)
    a2t = np.ascontiguousarray(lA.transpose(2, 0, 1).reshape(D, 128))
    b2 = np.ascontiguousarray(lB.transpose(0, 2, 1).reshape(128, D))

    nc = build_nc()
    in_maps = []
    for i in range(NCORES):
        xc = np.ascontiguousarray(x[:, i * CSL:(i + 1) * CSL, :].reshape(T, D))
        in_maps.append({"x": xc, "wT": wT, "gw": gw, "bias": bb,
                        "a2t": a2t, "b2": b2})
    resu = run_bass_kernel_spmd(nc, in_maps, core_ids=list(range(NCORES)))
    out = np.empty((B, SEQ, D), dtype=np.float32)
    for i in range(NCORES):
        out[:, i * CSL:(i + 1) * CSL, :] = np.asarray(
            resu.results[i]["y"]).reshape(B, CSL, D)
    return out
